# version 15
# baseline (speedup 1.0000x reference)
import os
import numpy as np
import ml_dtypes
from contextlib import ExitStack
import concourse.bass as bass
import concourse.mybir as mybir
from concourse.bass_utils import run_bass_kernel_spmd

F32 = mybir.dt.float32
BF16 = mybir.dt.bfloat16
I32 = mybir.dt.int32
AF = mybir.ActivationFunctionType
ALU = mybir.AluOpType
AX = mybir.AxisListType

NCORE = 8
L, D, NH, HF, CS = 16384, 2048, 32, 64, 16
TOKC = L // NCORE
TSC = 128
NSC = TOKC // TSC
HALF = TOKC // 2
SC_H = NSC // 2
NPAIR = NH // 2
NDC = D // 128
NIC = 8192 // 128
QTR = 512
NTOK = TOKC // 128
bfl = ml_dtypes.bfloat16

CB_I128, CB_NEGI128 = 0, 128
CB_STRICT, CB_STRICT_HI, CB_DIAG, CB_DIAG_HI = 256, 384, 512, 640
CB_N128ROW, CB_ONE, CB_N128COL = 768, 896, 897
CF_INVK, CF_NEGINVK, CF_EPS, CF_INVD = 0, 1, 2, 3

R0C = 16384
R1C = 32768


class Plan:
    ENGS = ("tensor", "vector", "scalar", "gpsimd", "sync")

    def __init__(self):
        self.ops = {e: [] for e in self.ENGS}
        self.cnt = {}
        self.waited = {e: {} for e in self.ENGS}
        self.guard = {}

    def inc(self, s, amt):
        v = self.cnt.get(s, 0) + amt
        self.cnt[s] = v
        return {s: v}

    def wait(self, eng, tok):
        if not tok:
            return
        for s, v in tok.items():
            if v > 0 and self.waited[eng].get(s, 0) < v:
                self.waited[eng][s] = v
                self.ops[eng].append(("wait", s, v))

    def tok(self, *sems):
        return {s: self.cnt.get(s, 0) for s in sems}

    def g(self, key):
        if key not in self.guard:
            self.guard[key] = {"w": {}, "r": {}}
        return self.guard[key]


def merge(*toks):
    out = {}
    for t in toks:
        if t:
            for s, v in t.items():
                out[s] = max(out.get(s, 0), v)
    return out


def build_program(phases=99):
    nc = bass.Bass("TRN2", target_bir_lowering=False, debug=False)

    x_in = nc.dram_tensor("x", [TOKC, D], F32, kind="ExternalInput").ap()
    wq = nc.dram_tensor("wq", [16, 128, 2048], F32, kind="ExternalInput").ap()
    wk = nc.dram_tensor("wk", [16, 128, 2048], F32, kind="ExternalInput").ap()
    wv = nc.dram_tensor("wv", [16, 128, 2048], F32, kind="ExternalInput").ap()
    wo = nc.dram_tensor("wo", [16, 128, 2048], F32, kind="ExternalInput").ap()
    wilr_d = nc.dram_tensor("wilr", [128, 512], F32, kind="ExternalInput").ap()
    bilr_d = nc.dram_tensor("bilr", [32, 1], F32, kind="ExternalInput").ap()
    w1_d = nc.dram_tensor("w1", [128, 1024], F32, kind="ExternalInput").ap()
    guw = nc.dram_tensor("guw", [64, 128, 4096], I32, kind="ExternalInput").ap()
    dw = nc.dram_tensor("dw", [64, 128, 2048], I32, kind="ExternalInput").ap()
    gs_d = nc.dram_tensor("gs_t", [128, 64], F32, kind="ExternalInput").ap()
    us_d = nc.dram_tensor("us_t", [128, 64], F32, kind="ExternalInput").ap()
    ds_d = nc.dram_tensor("ds_b", [128, 2048], F32, kind="ExternalInput").ap()
    cb_d = nc.dram_tensor("cb", [128, 904], BF16, kind="ExternalInput").ap()
    cf_d = nc.dram_tensor("cf", [128, 8], F32, kind="ExternalInput").ap()
    i64_d = nc.dram_tensor("i64f", [128, 96], F32, kind="ExternalInput").ap()
    pm_d = nc.dram_tensor("pmask", [128, 16], F32, kind="ExternalInput").ap()
    y_out = nc.dram_tensor("y", [TOKC, D], F32, kind="ExternalOutput").ap()

    xn_sp = nc.dram_tensor("xn_sp", [TOKC, D], BF16)
    g_sp = nc.dram_tensor("g_sp", [TOKC, D], BF16)
    xb_sp = nc.dram_tensor("xb_sp", [2, 128, 16 * HALF], BF16)
    xa_sp = nc.dram_tensor("xa_sp", [2, 128, 16 * HALF], BF16)
    s_sp = nc.dram_tensor("s_sp", [D, TOKC], BF16)
    h_sp = nc.dram_tensor("h_sp", [64, 128, 2048], BF16)
    cc_in = nc.dram_tensor("cc_in", [128, NPAIR * 128], F32)
    cc_out = nc.dram_tensor("cc_out", [NCORE * 128, NPAIR * 128], F32)

    p = Plan()

    es = ExitStack()
    with es:
        cbS = es.enter_context(nc.sbuf_tensor("consts_b", [128, 904], BF16))
        cfS = es.enter_context(nc.sbuf_tensor("consts_f", [128, 8], F32))
        idf = es.enter_context(nc.sbuf_tensor("ident64f", [128, 96], F32))
        pmS = es.enter_context(nc.sbuf_tensor("pmasks", [128, 16], F32))
        wilrS = es.enter_context(nc.sbuf_tensor("wilr_sb", [128, 512], BF16))
        bilrS = es.enter_context(nc.sbuf_tensor("bilr_sb", [32, 1], F32))
        S_all = es.enter_context(nc.sbuf_tensor("S_all", [128, NPAIR * 64], F32))
        Arun = es.enter_context(nc.sbuf_tensor("Arun", [128, NPAIR * 64], F32))
        Brun = es.enter_context(nc.sbuf_tensor("Brun", [128, NPAIR * 64], F32))
        ilrT = es.enter_context(nc.sbuf_tensor("ilrT", [128, 512], F32))
        rsrow = es.enter_context(nc.sbuf_tensor("rs_row", [1, TOKC], BF16))
        gscol = es.enter_context(nc.sbuf_tensor("gscol", [128, 64], F32))
        uscol = es.enter_context(nc.sbuf_tensor("uscol", [128, 64], F32))

        arena = es.enter_context(nc.sbuf_tensor("arena", [128, R0C + R1C], BF16))
        wbufs = es.enter_context(nc.sbuf_tensor("wbufs", [128, 2 * 4096], BF16))
        wdbuf = es.enter_context(nc.sbuf_tensor("wdbuf", [128, 2 * 2048], BF16))
        htbuf = es.enter_context(nc.sbuf_tensor("htbuf", [128, 2 * 512], BF16))
        gstg = es.enter_context(nc.sbuf_tensor("gstg", [128, 4 * 512], BF16))
        xstage = es.enter_context(nc.sbuf_tensor("xstage", [128, 2 * 2048], F32))
        stage_a = es.enter_context(nc.sbuf_tensor("stage_a", [128, 512], F32))
        ccbv = es.enter_context(nc.sbuf_tensor("ccbuf", [128, 2048], F32))
        colbuf = es.enter_context(nc.sbuf_tensor("colbuf", [128, 64], F32))
        hscorr = es.enter_context(nc.sbuf_tensor("hscorr", [128, 4], BF16))
        hsrow = es.enter_context(nc.sbuf_tensor("hsrow", [1, 512], BF16))
        rscolb = es.enter_context(nc.sbuf_tensor("rscol_b", [128, 1], BF16))
        MTs = es.enter_context(nc.sbuf_tensor("MTs", [128, 2 * 512], BF16))
        ATs_sb = es.enter_context(nc.sbuf_tensor("ATsb", [128, 2 * 512], BF16))
        ATd_sb = es.enter_context(nc.sbuf_tensor("ATdb", [128, 2 * 512], BF16))
        Z1lo = es.enter_context(nc.sbuf_tensor("Z1lo", [128, 2 * 256], BF16))
        Z1hi = es.enter_context(nc.sbuf_tensor("Z1hi", [128, 2 * 256], BF16))
        XBcs = es.enter_context(nc.sbuf_tensor("XBcs", [128, 2 * 256], BF16))
        SbdT = es.enter_context(nc.sbuf_tensor("Sbd", [128, 4 * 128], BF16))
        OUTsb = es.enter_context(nc.sbuf_tensor("OUTsb", [128, 2 * 128], BF16))
        cotokf = es.enter_context(nc.sbuf_tensor("cotokf", [128, 2 * 32], F32))
        ncotokf = es.enter_context(nc.sbuf_tensor("ncotokf", [128, 2 * 32], F32))
        cothi = es.enter_context(nc.sbuf_tensor("cothi", [64, 2 * 32], F32))
        ncothi = es.enter_context(nc.sbuf_tensor("ncothi", [64, 2 * 32], F32))
        colo64 = es.enter_context(nc.sbuf_tensor("colo64", [128, 2 * 32], F32))
        ncolo64 = es.enter_context(nc.sbuf_tensor("ncolo64", [128, 2 * 32], F32))
        dTs = es.enter_context(nc.sbuf_tensor("dTs", [128, 4 * 64], F32))
        ttmp = es.enter_context(nc.sbuf_tensor("ttmp", [128, 4 * 64], F32))
        odst = es.enter_context(nc.sbuf_tensor("odst", [128, 2 * 128], F32))
        pb0 = es.enter_context(nc.psum_tensor("pb0", [128, 512], F32))
        pb1 = es.enter_context(nc.psum_tensor("pb1", [128, 512], F32))
        pb2 = es.enter_context(nc.psum_tensor("pb2", [128, 512], F32))
        pb3 = es.enter_context(nc.psum_tensor("pb3", [128, 512], F32))
        pb4 = es.enter_context(nc.psum_tensor("pb4", [128, 512], F32))
        pb5 = es.enter_context(nc.psum_tensor("pb5", [128, 512], F32))
        pb6 = es.enter_context(nc.psum_tensor("pb6", [128, 512], F32))
        pb7 = es.enter_context(nc.psum_tensor("pb7", [128, 512], F32))
        SG = es.enter_context(nc.semaphore("sg"))
        SS = es.enter_context(nc.semaphore("ss"))
        SX = es.enter_context(nc.semaphore("sx"))
        PESEM = es.enter_context(nc.semaphore("pe"))
        DVSEM = es.enter_context(nc.semaphore("dv"))
        ACSEM = es.enter_context(nc.semaphore("ac"))
        GCSEM = es.enter_context(nc.semaphore("gc"))
        CCSEM = es.enter_context(nc.semaphore("cc"))
        SEM = {"sg": SG, "ss": SS, "sx": SX, "pe": PESEM, "dv": DVSEM,
               "ac": ACSEM, "gc": GCSEM, "cc": CCSEM}
        for _nm in ("WB0", "WB1", "WB2", "WD0", "WD1", "XS0", "XS1",
                    "SP0", "SP1", "SP2", "SP3", "CG0", "CG1", "CS0", "CS1",
                    "GS0", "GS1", "GS2", "GS3", "CCA", "CCB", "CCIN",
                    "SPB", "SPA", "RLD", "Y0", "Y1", "H0", "H1", "HT0", "HT1"):
            SEM[_nm] = es.enter_context(nc.semaphore(_nm))
        PBS = [pb0, pb1, pb2, pb3, pb4, pb5, pb6, pb7]

        def dma(eng, sem, out, in_, transpose=False, reads=(), writes=()):
            for k in writes:
                p.wait(eng, merge(p.g(k)["r"], p.g(k)["w"]))
            for k in reads:
                p.wait(eng, p.g(k)["w"])
            t = p.inc(sem, 16)
            def fn(e, out=out, in_=in_, tp=transpose, sem=sem):
                e.dma_start(out=out, in_=in_, transpose=tp).then_inc(SEM[sem], 16)
            p.ops[eng].append(("dma", sem, fn))
            for k in writes:
                p.g(k)["w"] = merge(p.g(k)["w"], t)
            for k in reads:
                p.g(k)["r"] = merge(p.g(k)["r"], t)
            return t

        bank_writes = {b: [] for b in range(8)}
        PSUM_IDX = {"pb%d" % i: i for i in range(8)}

        def _prange(apx):
            st, cnt = apx.ap[0]
            return (apx.offset // st, apx.offset // st + cnt)

        def _halves(pr_):
            out = []
            if pr_[0] < 64:
                out.append(0)
            if pr_[1] > 64:
                out.append(1)
            return out

        def _psum_aps(args):
            found = []
            for a in args:
                try:
                    nm = a.tensor.name
                except Exception:
                    continue
                if nm in PSUM_IDX:
                    found.append((PSUM_IDX[nm], _prange(a)))
            return found

        def mm(bank, out, lhsT, rhs, start=True, stop=True, tp=None,
               reads=(), wait=None):
            ohs = _halves(_prange(out))
            for h_ in ohs:
                p.wait("tensor", p.g(("pb", bank, h_))["r"])
            for k in reads:
                p.wait("tensor", p.g(k)["w"])
            if wait:
                p.wait("tensor", wait)
            rows, outp = _prange(lhsT), _prange(out)
            waited_pe = p.waited["tensor"].get("pe", 0)
            live = []
            for (r2, o2, t2) in bank_writes[bank]:
                if t2 <= waited_pe:
                    continue
                if (rows[1] <= r2[0] or r2[1] <= rows[0]) and \
                        not (outp[1] <= o2[0] or o2[1] <= outp[0]):
                    p.wait("tensor", {"pe": t2})
                    waited_pe = max(waited_pe, t2)
                else:
                    live.append((r2, o2, t2))
            bank_writes[bank] = [e for e in live if e[2] > waited_pe]
            t = p.inc("pe", 1)
            def fn(e, out=out, lhsT=lhsT, rhs=rhs, start=start, stop=stop, tp=tp):
                e.matmul(out, lhsT, rhs, start=start, stop=stop,
                         skip_group_check=True, tile_position=tp).then_inc(
                             SEM["pe"], 1)
            p.ops["tensor"].append(fn)
            bank_writes[bank].append((rows, outp, t["pe"]))
            for h_ in ohs:
                g2 = p.g(("pb", bank, h_))
                g2["w"] = merge(g2["w"], t)
            for k in reads:
                p.g(k)["r"] = merge(p.g(k)["r"], t)
            return t

        ENG_SEM = {"vector": "dv", "scalar": "ac", "gpsimd": "gc"}

        def vop(eng, meth, *args, banks=(), reads=(), writes=(), wait=None,
                **kwargs):
            pa = _psum_aps(list(args) + list(kwargs.values()))
            for b, pr_ in pa:
                for h_ in _halves(pr_):
                    p.wait(eng, p.g(("pb", b, h_))["w"])
            for k in reads:
                p.wait(eng, p.g(k)["w"])
            for k in writes:
                p.wait(eng, merge(p.g(k)["r"], p.g(k)["w"]))
            if wait:
                p.wait(eng, wait)
            s = ENG_SEM[eng]
            t = p.inc(s, 1)
            def fn(e, meth=meth, args=args, kwargs=kwargs, s=s):
                getattr(e, meth)(*args, **kwargs).then_inc(SEM[s], 1)
            p.ops[eng].append(fn)
            for b, pr_ in pa:
                for h_ in _halves(pr_):
                    g2 = p.g(("pb", b, h_))
                    g2["r"] = merge(g2["r"], t)
                bank_writes[b] = [e for e in bank_writes[b]
                                  if e[1][1] <= pr_[0] or pr_[1] <= e[1][0]]
            for k in reads:
                p.g(k)["r"] = merge(p.g(k)["r"], t)
            for k in writes:
                p.g(k)["w"] = merge(p.g(k)["w"], t)
            return t

        def copy_ev(eng, out, in_, banks=(), reads=(), writes=(), wait=None):
            meth = "tensor_copy" if eng in ("vector", "gpsimd") else "copy"
            return vop(eng, meth, out, in_, banks=banks, reads=reads,
                       writes=writes, wait=wait)

        def barrier():
            t = p.tok(*SEM.keys())
            for e in Plan.ENGS:
                p.wait(e, t)
            return t

        def ar(off, cols):
            return arena[:, off:off + cols]

        XS = lambda par: xstage[:, 2048 * par:2048 * par + 2048]
        STA_F = stage_a[:, 0:512]
        STA_C = stage_a[:, 0:512]
        CCB = ccbv

        t0 = [
            dma("gpsimd", "sg", cbS[:], cb_d),
            dma("gpsimd", "sg", cfS[:], cf_d),
            dma("gpsimd", "sg", idf[:], i64_d),
            dma("gpsimd", "sg", pmS[:], pm_d),
            dma("gpsimd", "sg", wilrS[:], wilr_d),
            dma("gpsimd", "sg", bilrS[:], bilr_d),
            dma("gpsimd", "sg", S_all[:], w1_d, writes=(("sall"),)),
            dma("gpsimd", "sg", gscol[:], gs_d),
            dma("gpsimd", "sg", uscol[:], us_d),
        ]
        t_consts = merge(*t0)
        p.wait("vector", t_consts)
        for pr in range(NPAIR):
            for hh in range(2):
                copy_ev("vector", Arun[64 * hh:64 * hh + 64, 64 * pr:64 * pr + 64],
                        idf[64 * hh:64 * hh + 64, 0:64], writes=(("arun"),))
        vop("vector", "memset", Brun[:], 0.0, writes=(("brun"),))
        vop("vector", "memset", SbdT[:], 0.0, writes=(("sbd", 0), ("sbd", 1)))

        for tt in range(NTOK if phases >= 1 else 0):
            par = tt % 2
            slot = tt % 4
            xs = XS(par)
            xn_t = ar(2048 * slot, 2048)
            t_x = dma("sync", "XS%d" % par, xs, x_in[128 * tt:128 * tt + 128, :],
                      writes=(("xst", par),))
            PB = 16 * par
            p.wait("vector", t_x)
            tst = None
            for c in range(4):
                tst = vop("vector", "bn_stats", colbuf[:, 32 + 6 * c:38 + 6 * c],
                          xs[:, 512 * c:512 * c + 512], reads=(("xst", par),))
            tv = vop("vector", "bn_aggr", colbuf[:, PB:PB + 2],
                     colbuf[:, 32:56], writes=(("cb", par),), wait=tst)
            ta = vop("scalar", "activation", colbuf[:, PB + 3:PB + 4],
                     colbuf[:, PB + 1:PB + 2], AF.Sqrt,
                     bias=cfS[:, CF_EPS:CF_EPS + 1], wait=tv,
                     reads=(("cb", par),), writes=(("cb", par),))
            ta = vop("vector", "reciprocal", colbuf[:, PB + 2:PB + 3],
                     colbuf[:, PB + 3:PB + 4], wait=ta,
                     reads=(("cb", par),), writes=(("cb", par),))
            tn = vop("vector", "tensor_scalar", xn_t, xs,
                     colbuf[:, PB:PB + 1], colbuf[:, PB + 2:PB + 3],
                     ALU.subtract, ALU.mult,
                     reads=(("xst", par), ("cb", par)), writes=(("r0", slot),),
                     wait=ta)
            tr = vop("vector", "tensor_reduce", colbuf[:, PB + 4:PB + 5], xn_t,
                     AX.X, ALU.add, reads=(("r0", slot),),
                     writes=(("cb", par),))
            trb = copy_ev("vector", rscolb[:], colbuf[:, PB + 4:PB + 5],
                          wait=tr, reads=(("cb", par),), writes=(("rscb"),))
            mm(5, pb5[0:1, 0:128], rscolb[:], cbS[:, CB_I128:CB_I128 + 128],
               reads=(("rscb"),))
            copy_ev("vector", rsrow[0:1, 128 * tt:128 * tt + 128],
                    pb5[0:1, 0:128], banks=(5,), writes=(("rsrow"),))
            dma("sync", "SP%d" % slot, xn_sp.ap()[128 * tt:128 * tt + 128, :],
                xn_t, reads=(("r0", slot),))
        barrier()

        def build_xnT(tok0, ntok, tile_cols, dst_base=0):
            toks = []
            for dc in range(NDC):
                toks.append(dma("scalar", "sx",
                                ar(dst_base + tile_cols * dc, ntok),
                                xn_sp.ap()[tok0:tok0 + ntok, 128 * dc:128 * dc + 128],
                                transpose=True, writes=(("xnq"),)))
            return merge(*toks)

        HT_O = R0C

        def ffn_gu_gen(qq, xbase):
            for ic in range(NIC):
                slot = ic % 2
                dma("gpsimd", "WB%d" % slot,
                    wbufs[:, 4096 * slot:4096 * slot + 4096], guw[ic],
                    writes=(("wb", slot),))
                gbuf = wbufs[:, 4096 * slot:4096 * slot + 2048]
                ubuf = wbufs[:, 4096 * slot + 2048:4096 * slot + 4096]
                for dc in range(NDC):
                    mm(4, pb4[:, 0:QTR], gbuf[:, 128 * dc:128 * dc + 128],
                       ar(xbase + 512 * dc, QTR), start=(dc == 0), stop=False,
                       reads=(("wb", slot), ("xnq")) if dc == 0 else (("wb", slot),))
                for dc in range(NDC):
                    mm(5, pb5[:, 0:QTR], ubuf[:, 128 * dc:128 * dc + 128],
                       ar(xbase + 512 * dc, QTR), start=(dc == 0), stop=False,
                       reads=(("wb", slot),))
                rs = rsrow[0:1, QTR * qq:QTR * qq + QTR]
                mm(4, pb4[:, 0:QTR], cbS[0:1, CB_N128ROW:CB_N128ROW + 128],
                   rs, start=False, stop=True)
                mm(5, pb5[:, 0:QTR], cbS[0:1, CB_N128ROW:CB_N128ROW + 128],
                   rs, start=False, stop=True)
                yield
                ta = vop("scalar", "activation", stage_a[:, 0:512],
                         pb4[:, 0:QTR], AF.Silu, scale=gscol[:, ic:ic + 1],
                         writes=(("silu"),))
                hsl = ic % 2
                h_t = wdbuf[:, 1024 * hsl:1024 * hsl + 512]
                vop("vector", "scalar_tensor_tensor", h_t, pb5[:, 0:QTR],
                    uscol[:, ic:ic + 1], stage_a[:, 0:512], ALU.mult, ALU.mult,
                    reads=(("silu"),), writes=(("hstg", hsl),), wait=ta)
                dma("sync", "H%d" % hsl,
                    h_sp.ap()[ic, :, 512 * qq:512 * qq + 512], h_t,
                    reads=(("hstg", hsl),))
                yield

        def ffn_down(qq):
            t_ds = dma("sync", "ss", xstage[:, 2048:4096], ds_d,
                       writes=(("dsbx"),))
            for dq in range(4):
                for ic in range(NIC):
                    icg = ic // 4
                    igo = ic % 4
                    wslot = icg % 2
                    if igo == 0:
                        dma("gpsimd", "WD%d" % wslot,
                            wdbuf[:, 2048 * wslot:2048 * wslot + 2048]
                            .rearrange("p (i c) -> p i c", c=512),
                            dw[4 * icg:4 * icg + 4, :, 512 * dq:512 * dq + 512]
                            .transpose([1, 0, 2]),
                            writes=(("wd", wslot), ("hstg", 0), ("hstg", 1)))
                    hts = ic % 2
                    dma("sync", "HT%d" % hts,
                        htbuf[:, 512 * hts:512 * hts + 512],
                        h_sp.ap()[ic, :, 512 * qq:512 * qq + 512],
                        writes=(("htb", hts),))
                    wbd = wdbuf[:, 2048 * wslot + 512 * igo:
                                2048 * wslot + 512 * igo + 512]
                    if dq == 0:
                        mm(6, pb6[0:1, 0:QTR],
                           cbS[:, CB_N128COL:CB_N128COL + 1],
                           htbuf[:, 512 * hts:512 * hts + 512],
                           start=(ic == 0), stop=(ic == NIC - 1),
                           reads=(("htb", hts),))
                    for tk in range(4):
                        mm(tk, PBS[tk][:, 0:512],
                           htbuf[:, 512 * hts + 128 * tk:512 * hts + 128 * tk + 128],
                           wbd, start=(ic == 0), stop=(ic == NIC - 1),
                           reads=(("wd", wslot), ("htb", hts)))
                if dq == 0:
                    th = copy_ev("vector", hsrow[0:1, 0:QTR], pb6[0:1, 0:QTR],
                                 writes=(("hsr"),))
                    for k in range(4):
                        mm(7, pb7[0:128, 4 + k:5 + k],
                           hsrow[0:1, 128 * k:128 * k + 128],
                           cbS[0:1, CB_ONE:CB_ONE + 1], start=True,
                           reads=(("hsr"),))
                    copy_ev("vector", hscorr[:, 0:4], pb7[0:128, 4:8],
                            writes=(("hsc"),))
                for tk in range(4):
                    tv = vop("vector", "scalar_tensor_tensor", STA_F,
                             PBS[tk][:, 0:512], hscorr[:, tk:tk + 1],
                             xstage[:, 2048 + 512 * dq:2048 + 512 * dq + 512],
                             ALU.add, ALU.mult,
                             reads=(("hsc"), ("dsbx")), writes=(("sta"),))
                    g_st = gstg[:, 512 * tk:512 * tk + 512]
                    ta = vop("scalar", "activation", g_st, STA_F,
                             AF.Silu, reads=(("sta"),), writes=(("gst", tk),),
                             wait=tv)
                    dma("sync", "GS%d" % tk,
                        g_sp.ap()[QTR * qq + 128 * tk:QTR * qq + 128 * tk + 128,
                                  512 * dq:512 * dq + 512], g_st,
                        reads=(("gst", tk),))


        def project(wap, dst_off, xn_base=0, n_tok=HALF, wap_base=0):
            ncol = n_tok // 512
            for oc in range(16):
                b0, b1 = (0, 1) if oc % 2 == 0 else (2, 3)
                wslot = (oc // 2) % 2
                if oc % 2 == 0:
                    dma("gpsimd", "WB%d" % wslot,
                        wbufs[:, 4096 * wslot:4096 * wslot + 4096]
                        .rearrange("p (i c) -> p i c", c=2048),
                        wap.tensor.ap()[wap_base + oc:wap_base + oc + 2]
                        .transpose([1, 0, 2]),
                        writes=(("wb", wslot),))
                wbuf = wbufs[:, 4096 * wslot + 2048 * (oc % 2):
                             4096 * wslot + 2048 * (oc % 2) + 2048]
                for cc2 in range(ncol):
                    bnk = (b0, b1)[cc2 % 2]
                    for dc in range(NDC):
                        mm(bnk, PBS[bnk][:, 0:512],
                           wbuf[:, 128 * dc:128 * dc + 128],
                           ar(xn_base + n_tok * dc + 512 * cc2, 512),
                           start=(dc == 0), stop=(dc == NDC - 1),
                           reads=(("wb", wslot), ("xnq")))
                    eng = "vector" if (oc + cc2) % 2 == 0 else "scalar"
                    copy_ev(eng, ar(dst_off + n_tok * oc + 512 * cc2, 512),
                            PBS[bnk][:, 0:512], banks=(bnk,),
                            writes=(("proj"),))

        def ilr_proj(half):
            for c2 in range(2):
                b = 2 * half + c2
                for dc in range(NDC):
                    mm(4, pb4[0:32, 0:512], wilrS[:, 32 * dc:32 * dc + 32],
                       ar(1024 * dc + 512 * c2, 512), start=(dc == 0),
                       stop=(dc == NDC - 1), reads=(("xnq"),))
                vop("scalar", "activation",
                    ilrT[32 * b:32 * b + 32, 0:512],
                    pb4[0:32, 0:512], AF.Sigmoid, bias=bilrS[:, 0:1], banks=(4,),
                    writes=(("ilrt"),))

        def co_build(half, s):
            sl = s % 2
            base = HALF * half + TSC * s
            b, boff = base // 512, base % 512
            tpb = (32 * b, 0) if b == 3 else None
            mm(6, pb6[0:128, 448:480],
               ilrT[32 * b:32 * b + 32, boff:boff + 128],
               idf[32 * b:32 * b + 32, 64:96], reads=(("ilrt"),), tp=tpb)
            mm(6, pb6[0:64, 480:512],
               ilrT[32 * b:32 * b + 32, boff + 64:boff + 128],
               idf[32 * b:32 * b + 32, 64:96], start=True, tp=tpb)
            vop("vector", "tensor_scalar", cotokf[:, 32 * sl:32 * sl + 32],
                pb6[0:128, 448:480], cfS[:, CF_INVK:CF_INVK + 1],
                None, ALU.mult, banks=(6,), writes=(("co", sl),))
            vop("vector", "tensor_scalar", ncotokf[:, 32 * sl:32 * sl + 32],
                pb6[0:128, 448:480],
                cfS[:, CF_NEGINVK:CF_NEGINVK + 1], None, ALU.mult, banks=(6,),
                writes=(("co", sl),))
            vop("vector", "tensor_scalar", cothi[:, 32 * sl:32 * sl + 32],
                pb6[0:64, 480:512],
                cfS[0:64, CF_INVK:CF_INVK + 1], None, ALU.mult, banks=(6,),
                writes=(("co", sl),))
            vop("vector", "tensor_scalar", ncothi[:, 32 * sl:32 * sl + 32],
                pb6[0:64, 480:512],
                cfS[0:64, CF_NEGINVK:CF_NEGINVK + 1], None, ALU.mult, banks=(6,),
                writes=(("co", sl),))
            mm(6, pb6[64:128, 480:512],
               ilrT[32 * b:32 * b + 32, boff:boff + 64],
               idf[32 * b:32 * b + 32, 64:96], start=True, tp=(32 * b, 64))
            vop("vector", "tensor_scalar", colo64[64:128, 32 * sl:32 * sl + 32],
                pb6[64:128, 480:512], cfS[64:128, CF_INVK:CF_INVK + 1], None,
                ALU.mult, writes=(("co", sl),))
            vop("vector", "tensor_scalar", ncolo64[64:128, 32 * sl:32 * sl + 32],
                pb6[64:128, 480:512], cfS[64:128, CF_NEGINVK:CF_NEGINVK + 1],
                None, ALU.mult, writes=(("co", sl),))

        SUB_STEPS = [
            (0, 0, 16, 0, 0, 32, 0, 0, 32),
            (0, 0, 32, 0, 32, 64, 0, 32, 64),
            (0, 0, 32, 1, 0, 64, 0, 64, 128),
            (0, 32, 48, 0, 32, 64, 0, 32, 64),
            (0, 32, 64, 1, 0, 64, 0, 64, 128),
            (1, 0, 16, 1, 0, 32, 1, 64, 96),
            (1, 0, 32, 1, 32, 64, 1, 96, 128),
            (1, 32, 48, 1, 32, 64, 1, 96, 128),
        ]

        KSP = int(os.environ.get("KSP", "9"))

        def scan_pair_gen(pr, s, half, phase):
            par = pr % 2
            rh = (pr // 2) % 2
            RB = 64 * rh
            sl = s % 2
            slot4 = 2 * par + rh
            bP_lo = bP_hi = slot4
            bGA = slot4
            bT = 6 if par == 0 else 7
            if phase == "A":
                sc0 = TSC * s
                xbt = lambda c0, c1: ar(R0C + 1024 * pr + sc0 + c0, c1 - c0)
                xat = lambda c0, c1: ar(R0C + 16384 + 1024 * pr + sc0 + c0,
                                        c1 - c0)
                xct = None
            else:
                sc0 = TSC * (s % 4)
                xbt = lambda c0, c1: ar(R0C + 512 * pr + sc0 + c0, c1 - c0)
                xat = lambda c0, c1: ar(R0C + 8192 + 512 * pr + sc0 + c0,
                                        c1 - c0)
                xct = lambda c0, c1: ar(R0C + 16384 + 512 * pr + sc0 + c0,
                                        c1 - c0)
            hb = 128 if phase == "A" else 64
            pw = 2 * hb
            P_lo = PBS[bP_lo][RB:RB + 64, 0:pw]
            P_hi = PBS[bP_hi][RB:RB + 64, pw:2 * pw]
            zrow = cbS[32 + RB:33 + RB, CB_N128ROW:CB_N128ROW + 128]

            def zfill(bank, out_ap, ncols):
                ob = _prange(out_ap)[0]
                mm(bank, out_ap, zrow[:, 0:out_ap.shape[0]],
                   cbS[32 + RB:33 + RB, 0:ncols], start=True, stop=False,
                   tp=(32 + RB, ob))
            mts = lambda tl, hh: MTs[RB:RB + 64,
                                     512 * par + 128 * (2 * tl + hh):
                                     512 * par + 128 * (2 * tl + hh) + 128]
            z1 = lambda tl: (Z1lo if tl == 0 else Z1hi)[RB:RB + 64,
                                                        256 * par:256 * par + pw]
            cx = 32 * sl + 2 * pr + hh if False else None
            col = lambda hh: 32 * sl + 2 * pr + hh
            if rh == 0:
                co_lo = lambda hh: cotokf[0:64, col(hh):col(hh) + 1]
                nco_lo = lambda hh: ncotokf[0:64, col(hh):col(hh) + 1]
                co_hi2 = lambda hh: cothi[0:64, col(hh):col(hh) + 1]
                nco_hi2 = lambda hh: ncothi[0:64, col(hh):col(hh) + 1]
            else:
                co_lo = lambda hh: colo64[64:128, col(hh):col(hh) + 1]
                nco_lo = lambda hh: ncolo64[64:128, col(hh):col(hh) + 1]
                co_hi2 = lambda hh: cotokf[64:128, col(hh):col(hh) + 1]
                nco_hi2 = lambda hh: ncotokf[64:128, col(hh):col(hh) + 1]
            nco_full = lambda hh: ncotokf[:, col(hh):col(hh) + 1]
            msk_s = cbS[RB:RB + 64, CB_STRICT:CB_STRICT + 128]
            msk_sh = cbS[RB:RB + 64, CB_STRICT_HI:CB_STRICT_HI + 128]
            msk_d = cbS[RB:RB + 64, CB_DIAG:CB_DIAG + 128]
            msk_dh = cbS[RB:RB + 64, CB_DIAG_HI:CB_DIAG_HI + 128]
            K_MT = ("mts", par, rh)
            K_Z1 = ("z1", par, rh)
            K_XBC = ("xbc", par, rh)
            K_ATS = ("ats", par, rh)
            K_OSB = ("osb", par)
            K_SBD = ("sbd", slot4)

            for hh in range(2):
                r0 = 64 * hh
                mm(bGA, PBS[bGA][RB:RB + 64, 128 * hh:128 * hh + 128],
                   xbt(0, 64)[r0:r0 + 64, :], xbt(0, 128)[r0:r0 + 64, :],
                   start=True, reads=(("proj"),), tp=(r0, RB))
                mm(bGA, PBS[bGA][RB:RB + 64, 256 + 128 * hh:256 + 128 * hh + 128],
                   xbt(64, 128)[r0:r0 + 64, :], xbt(0, 128)[r0:r0 + 64, :],
                   start=True, tp=(r0, RB))
            for hh in range(2):
                vop("vector", "scalar_tensor_tensor", mts(0, hh),
                    PBS[bGA][RB:RB + 64, 128 * hh:128 * hh + 128], nco_lo(hh),
                    msk_s, ALU.mult, ALU.mult, reads=(("co", sl),),
                    writes=(K_MT,))
                vop("vector", "scalar_tensor_tensor", mts(1, hh),
                    PBS[bGA][RB:RB + 64, 256 + 128 * hh:256 + 128 * hh + 128],
                    nco_hi2(hh), msk_sh, ALU.mult, ALU.mult, writes=(K_MT,))

            yield
            if KSP < 2:
                return
            mm(bGA, PBS[bGA][RB:RB + 64, 0:128], xbt(0, 64),
               cbS[:, CB_I128:CB_I128 + 128], start=True, tp=(0, RB))
            mm(bGA, PBS[bGA][RB:RB + 64, 128:256], xbt(64, 128),
               cbS[:, CB_I128:CB_I128 + 128], start=True, tp=(0, RB))
            xbc = XBcs[RB:RB + 64, 256 * par:256 * par + 256]
            for hh in range(2):
                vop("vector", "tensor_scalar", xbc[0:64, 64 * hh:64 * hh + 64],
                    PBS[bGA][RB:RB + 64, 64 * hh:64 * hh + 64], co_lo(hh),
                    None, ALU.mult, writes=(K_XBC,))
                vop("vector", "tensor_scalar",
                    xbc[0:64, 128 + 64 * hh:128 + 64 * hh + 64],
                    PBS[bGA][RB:RB + 64, 128 + 64 * hh:128 + 64 * hh + 64],
                    co_hi2(hh), None, ALU.mult, writes=(K_XBC,))

            yield
            yield
            at_s = lambda tl, hh: ATs_sb[RB:RB + 64,
                                         512 * par + 128 * (2 * tl + hh):
                                         512 * par + 128 * (2 * tl + hh) + 128]
            at_d = lambda tl, hh: ATd_sb[RB:RB + 64,
                                         512 * par + 128 * (2 * tl + hh):
                                         512 * par + 128 * (2 * tl + hh) + 128]
            if phase == "B":
                for hh in range(2):
                    r0 = 64 * hh
                    mm(bGA, PBS[bGA][RB:RB + 64, 128 * hh:128 * hh + 128],
                       xbt(0, 64)[r0:r0 + 64, :], xct(0, 128)[r0:r0 + 64, :],
                       start=True, tp=(r0, RB))
                    mm(bGA, PBS[bGA][RB:RB + 64, 256 + 128 * hh:256 + 128 * hh + 128],
                       xbt(64, 128)[r0:r0 + 64, :], xct(0, 128)[r0:r0 + 64, :],
                       start=True, tp=(r0, RB))
                for hh in range(2):
                    vop("vector", "scalar_tensor_tensor", at_s(0, hh),
                        PBS[bGA][RB:RB + 64, 128 * hh:128 * hh + 128],
                        nco_lo(hh), msk_s, ALU.mult, ALU.mult, writes=(K_ATS,))
                    vop("vector", "tensor_tensor", at_d(0, hh),
                        PBS[bGA][RB:RB + 64, 128 * hh:128 * hh + 128],
                        msk_d, ALU.mult, writes=(K_ATS,))
                    vop("vector", "scalar_tensor_tensor", at_s(1, hh),
                        PBS[bGA][RB:RB + 64, 256 + 128 * hh:256 + 128 * hh + 128],
                        nco_hi2(hh), msk_sh, ALU.mult, ALU.mult, writes=(K_ATS,))
                    vop("vector", "tensor_tensor", at_d(1, hh),
                        PBS[bGA][RB:RB + 64, 256 + 128 * hh:256 + 128 * hh + 128],
                        msk_dh, ALU.mult, writes=(K_ATS,))
                sbd = SbdT[:, 128 * slot4:128 * slot4 + 128]
                copy_ev("vector", sbd[0:64, 0:64],
                        S_all[0:64, 64 * pr:64 * pr + 64],
                        reads=(("sall"),), writes=(K_SBD,))
                copy_ev("vector", sbd[64:128, 64:128],
                        S_all[64:128, 64 * pr:64 * pr + 64],
                        reads=(("sall"),), writes=(K_SBD,))
                zfill(bP_lo, PBS[bP_lo][RB:RB + 64, 0:2 * pw], 2 * pw)
                mm(bP_lo, P_lo, xbt(0, 64), sbd, start=False, stop=False,
                   reads=(K_SBD,), tp=(0, RB))
                mm(bP_lo, P_lo, xat(0, 64),
                   cbS[:, CB_NEGI128:CB_NEGI128 + 128], start=False, stop=False,
                   tp=(0, RB))
                mm(bP_hi, P_hi, xbt(64, 128), sbd, start=False, stop=False,
                   reads=(K_SBD,), tp=(0, RB))
                mm(bP_hi, P_hi, xat(64, 128),
                   cbS[:, CB_NEGI128:CB_NEGI128 + 128], start=False, stop=False,
                   tp=(0, RB))
            else:
                zfill(bP_lo, PBS[bP_lo][RB:RB + 64, 0:2 * pw], 2 * pw)
                for hh in range(2):
                    r0 = 64 * hh
                    i64b = cbS[r0:r0 + 64, CB_I128 + r0:CB_I128 + r0 + 64]
                    mm(bP_lo, P_lo[:, 128 * hh:128 * hh + 64],
                       xbt(0, 64)[r0:r0 + 64, :], i64b, start=False,
                       stop=False, reads=(("proj"),), tp=(r0, RB))
                    mm(bP_lo, P_lo[:, 128 * hh + 64:128 * hh + 128],
                       xat(0, 64)[r0:r0 + 64, :], i64b, start=False, stop=False,
                       tp=(r0, RB))
                    mm(bP_hi, P_hi[:, 128 * hh:128 * hh + 64],
                       xbt(64, 128)[r0:r0 + 64, :], i64b, start=False,
                       stop=False, tp=(r0, RB))
                    mm(bP_hi, P_hi[:, 128 * hh + 64:128 * hh + 128],
                       xat(64, 128)[r0:r0 + 64, :], i64b, start=False,
                       stop=False, tp=(r0, RB))

            yield
            if KSP < 3:
                return
            ei = [0]

            def evac32(tl, r0):
                src = (P_lo if tl == 0 else P_hi)[r0:r0 + 32, :]
                dst = z1(tl)[r0:r0 + 32, :]
                eng = "scalar" if (ei[0] + slot4) % 2 == 0 else "vector"
                ei[0] += 1
                return copy_ev(eng, dst, src, writes=(K_Z1,))

            def sub_step(i):
                tile_s, s0, s1, tile_d, d0, d1, mt_t, m0, m1 = SUB_STEPS[i]
                dstP = P_lo if tile_d == 0 else P_hi
                bank_d = bP_lo if tile_d == 0 else bP_hi
                for hh in range(2):
                    mm(bank_d, dstP[d0:d1, hb * hh:hb * hh + hb],
                       mts(mt_t, hh)[s0:s1, m0:m1],
                       z1(tile_s)[s0:s1, hb * hh:hb * hh + hb],
                       start=False, stop=False, reads=(K_MT, K_Z1),
                       tp=(RB + s0, RB + d0))
            evac32(0, 0)
            yield
            sub_step(0)
            evac32(0, 0)
            yield
            sub_step(1)
            sub_step(2)
            evac32(0, 32)
            yield
            sub_step(3)
            evac32(0, 32)
            yield
            sub_step(4)
            evac32(1, 0)
            yield
            sub_step(5)
            evac32(1, 0)
            yield
            sub_step(6)
            evac32(1, 32)
            yield
            sub_step(7)
            evac32(1, 32)
            yield

            if KSP < 5:
                return
            if phase == "A":
                zfill(bT, PBS[bT][0:128, 0:256], 256)
                for hh in range(2):
                    yl = z1(0)[:, 128 * hh:128 * hh + 64]
                    yh = z1(1)[:, 128 * hh:128 * hh + 64]
                    vl = z1(0)[:, 128 * hh + 64:128 * hh + 128]
                    vh = z1(1)[:, 128 * hh + 64:128 * hh + 128]
                    xcl = xbc[0:64, 64 * hh:64 * hh + 64]
                    xch = xbc[0:64, 128 + 64 * hh:128 + 64 * hh + 64]
                    mm(bT, PBS[bT][64 * hh:64 * hh + 64, 0:64], yl, xcl,
                       start=False, stop=False, reads=(K_XBC, K_Z1),
                       tp=(RB, 64 * hh))
                    mm(bT, PBS[bT][64 * hh:64 * hh + 64, 0:64], yh, xch,
                       start=False, stop=False, tp=(RB, 64 * hh))
                    mm(bT, PBS[bT][64 * hh:64 * hh + 64, 64:128], xcl, vl,
                       start=False, stop=False, tp=(RB, 64 * hh))
                    mm(bT, PBS[bT][64 * hh:64 * hh + 64, 64:128], xch, vh,
                       start=False, stop=False, tp=(RB, 64 * hh))
                if KSP < 6:
                    return
                dt = dTs[:, 64 * (2 * par + rh):64 * (2 * par + rh) + 64]
                copy_ev("vector", dt, PBS[bT][:, 0:64],
                        writes=(("dts", slot4),))
                arp = Arun[:, 64 * pr:64 * pr + 64]
                brp = Brun[:, 64 * pr:64 * pr + 64]
                for hh in range(2):
                    r0 = 64 * hh
                    mm(bT, PBS[bT][r0:r0 + 64, 128:192], dt[r0:r0 + 64, :],
                       arp[r0:r0 + 64, :], start=False, stop=False,
                       reads=(("dts", slot4), ("arun")), tp=(r0, r0))
                    mm(bT, PBS[bT][r0:r0 + 64, 192:256], dt[r0:r0 + 64, :],
                       brp[r0:r0 + 64, :], start=False, stop=False,
                       reads=(("brun"),), tp=(r0, r0))
                vop("vector", "tensor_sub", arp, arp, PBS[bT][:, 128:192],
                    writes=(("arun"),))
                vop("vector", "tensor_sub",
                    ttmp[:, 64 * slot4:64 * slot4 + 64],
                    brp, PBS[bT][:, 192:256], writes=(("ttm", slot4),))
                vop("vector", "tensor_add", brp,
                    ttmp[:, 64 * slot4:64 * slot4 + 64], PBS[bT][:, 64:128],
                    reads=(("ttm", slot4),), writes=(("brun"),))
            else:
                sbd = SbdT[:, 128 * slot4:128 * slot4 + 128]
                zfill(bT, PBS[bT][0:128, 0:448], 448)
                mm(bT, PBS[bT][0:128, 0:128], xct(0, 128), sbd,
                   start=False, stop=False, reads=(K_SBD,), tp=(0, 0))
                for hh in range(2):
                    mm(bT, PBS[bT][0:128, 64 * hh:64 * hh + 64], at_s(0, hh),
                       z1(0)[:, 64 * hh:64 * hh + 64], start=False, stop=False,
                       reads=(K_ATS, K_Z1), tp=(RB, 0))
                    mm(bT, PBS[bT][0:128, 64 * hh:64 * hh + 64], at_s(1, hh),
                       z1(1)[:, 64 * hh:64 * hh + 64], start=False, stop=False,
                       tp=(RB, 0))
                    mm(bT, PBS[bT][0:128, 128 + 64 * hh:128 + 64 * hh + 64],
                       at_d(0, hh), z1(0)[:, 64 * hh:64 * hh + 64],
                       start=False, stop=False, tp=(RB, 0))
                    mm(bT, PBS[bT][0:128, 128 + 64 * hh:128 + 64 * hh + 64],
                       at_d(1, hh), z1(1)[:, 64 * hh:64 * hh + 64],
                       start=False, stop=False, tp=(RB, 0))
                for hh in range(2):
                    mm(bT, PBS[bT][64 * hh:64 * hh + 64, 256:320],
                       xbc[0:64, 64 * hh:64 * hh + 64],
                       z1(0)[:, 64 * hh:64 * hh + 64], start=False, stop=False,
                       reads=(K_XBC,), tp=(RB, 64 * hh))
                    mm(bT, PBS[bT][64 * hh:64 * hh + 64, 256:320],
                       xbc[0:64, 128 + 64 * hh:128 + 64 * hh + 64],
                       z1(1)[:, 64 * hh:64 * hh + 64], start=False, stop=False,
                       tp=(RB, 64 * hh))
                od2 = odst[:, 128 * par:128 * par + 128]
                copy_ev("vector", od2, PBS[bT][:, 128:256],
                        writes=(("od", par),))
                osb = OUTsb[:, 128 * par:128 * par + 128]
                for hh in range(2):
                    vop("vector", "scalar_tensor_tensor",
                        osb[:, 64 * hh:64 * hh + 64],
                        od2[:, 64 * hh:64 * hh + 64], nco_full(hh),
                        PBS[bT][:, 64 * hh:64 * hh + 64], ALU.mult, ALU.add,
                        reads=(("od", par),), writes=(K_OSB,))
                vop("vector", "tensor_sub", S_all[:, 64 * pr:64 * pr + 64],
                    S_all[:, 64 * pr:64 * pr + 64], PBS[bT][:, 256:320],
                    writes=(("sall"),))
                mm(bT, PBS[bT][0:128, 320:448], osb,
                   cbS[:, CB_I128:CB_I128 + 128], start=False,
                   reads=(K_OSB,), tp=(0, 0))
                eng = "vector" if pr % 2 == 0 else "scalar"
                copy_ev(eng, ar(1024 * pr + TSC * s, 128), PBS[bT][:, 320:448],
                        writes=(("xcwt"),))

        def run_rolling(factories, filler=None):
            todo = list(factories)
            active = []
            while todo and len(active) < 4:
                active.append(todo.pop(0)())
            while active:
                nxt = []
                for gg in active:
                    try:
                        next(gg)
                        nxt.append(gg)
                    except StopIteration:
                        if todo:
                            g2 = todo.pop(0)()
                            try:
                                next(g2)
                                nxt.append(g2)
                            except StopIteration:
                                pass
                if filler:
                    for fg in list(filler):
                        try:
                            next(fg)
                        except StopIteration:
                            filler.remove(fg)
                active = nxt

        KP3 = int(os.environ.get("KP3", "9"))
        for half in range(2 if phases >= 3 else 0):
            build_xnT(HALF * half, HALF, 1024)
            project(wk, R0C)
            project(wv, R0C + 16384)
            ilr_proj(half)
            for s in range(SC_H if KP3 >= 2 else 0):
                co_build(half, s)
                if KP3 < 3:
                    continue
                np_ = NPAIR if KP3 >= 4 else 1
                run_rolling([lambda pr=pr: scan_pair_gen(pr, s, half, "A")
                             for pr in range(np_)])
            dma("sync", "SPB", xb_sp.ap()[half], ar(R0C, 16384),
                reads=(("proj"),))
            dma("sync", "SPA", xa_sp.ap()[half], ar(R0C + 16384, 16384),
                reads=(("proj"),))
        barrier()

        for pr in range(NPAIR if phases >= 4 else 0):
            for hh in range(2):
                r0 = 64 * hh
                mm(6, pb6[r0:r0 + 64, 0:64],
                   Arun[r0:r0 + 64, 64 * pr:64 * pr + 64], idf[r0:r0 + 64, 0:64],
                   start=True, reads=(("arun"),))
            copy_ev("vector", CCB[:, 128 * pr:128 * pr + 64],
                    pb6[:, 0:64], banks=(6,), writes=(("ccb"),))
            copy_ev("vector", CCB[:, 128 * pr + 64:128 * pr + 128],
                    Brun[:, 64 * pr:64 * pr + 64], reads=(("brun"),),
                    writes=(("ccb"),))
        if phases >= 4:
          dma("sync", "CCIN", cc_in.ap(), CCB[:, 0:2048], reads=(("ccb"),))
          p.wait("gpsimd", p.tok("CCIN"))
          t_cc = p.inc("cc", 1)
          def fn_cc(e):
              e.collective_compute("AllGather", ALU.bypass,
                                   replica_groups=[list(range(NCORE))],
                                   ins=[cc_in.ap().opt()],
                                   outs=[cc_out.ap().opt()]).then_inc(SEM["cc"], 1)
          p.ops["gpsimd"].append(fn_cc)
          p.wait("sync", p.tok("cc"))
          for j in range(NCORE - 1):
              tja = dma("sync", "CCA", CCB[:, 0:1024],
                        cc_out.ap()[128 * j:128 * j + 128, 0:1024],
                        writes=(("ccb"),))
              tjb = dma("sync", "CCB", CCB[:, 1024:2048],
                        cc_out.ap()[128 * j:128 * j + 128, 1024:2048],
                        writes=(("ccb"),))
              mj = pmS[:, j:j + 1]
              mj1 = pmS[:, 8 + j:8 + j + 1]
              for pr in range(NPAIR):
                  atp = CCB[:, 128 * pr:128 * pr + 64]
                  btp = CCB[:, 128 * pr + 64:128 * pr + 128]
                  sp = S_all[:, 64 * pr:64 * pr + 64]
                  mm(6, pb6[0:128, 64:128],
                     cbS[32:33, CB_N128ROW:CB_N128ROW + 128],
                     cbS[32:33, 0:64], start=True, stop=False, tp=(32, 0))
                  for hh in range(2):
                      r0 = 64 * hh
                      mm(6, pb6[r0:r0 + 64, 64:128], atp[r0:r0 + 64, :],
                         sp[r0:r0 + 64, :], start=False, stop=False,
                         reads=(("ccb"), ("sall")), tp=(r0, r0))
                      mm(6, pb6[r0:r0 + 64, 64:128], idf[r0:r0 + 64, 0:64],
                         btp[r0:r0 + 64, :], start=False, stop=False,
                         reads=(("ccb"),), tp=(r0, r0))
                  vop("vector", "tensor_scalar", ttmp[:, 0:64], sp, mj1, None,
                      ALU.mult, reads=(("sall"),), writes=(("ttm", 0),))
                  vop("vector", "scalar_tensor_tensor", sp, pb6[:, 64:128], mj,
                      ttmp[:, 0:64], ALU.mult, ALU.add, banks=(6,),
                      reads=(("ttm", 0),), writes=(("sall"),))
        barrier()

        for half in range(2 if phases >= 5 else 0):
            for qh in range(2):
                build_xnT(HALF * half + 512 * qh, 512, 512, R0C + 24576)
                project(wq, R0C + 16384, xn_base=R0C + 24576, n_tok=512)
                xbv = xb_sp.ap()[half].rearrange(
                    "p (oc c) -> p oc c", c=1024)[:, :, 512 * qh:512 * qh + 512]
                xav = xa_sp.ap()[half].rearrange(
                    "p (oc c) -> p oc c", c=1024)[:, :, 512 * qh:512 * qh + 512]
                dma("sync", "RLD",
                    ar(R0C, 8192).rearrange("p (oc c) -> p oc c", c=512),
                    xbv, writes=(("proj"),))
                dma("sync", "RLD",
                    ar(R0C + 8192, 8192).rearrange("p (oc c) -> p oc c", c=512),
                    xav, writes=(("proj"),))
                qq = 2 * half + qh
                filler = [ffn_gu_gen(qq, R0C + 24576)]
                for siq in range(4):
                    s = 4 * qh + siq
                    co_build(half, s)
                    run_rolling([lambda pr=pr: scan_pair_gen(pr, s, half, "B")
                                 for pr in range(NPAIR)], filler)
                for gg in filler:
                    for _ in gg:
                        pass
                ffn_down(qq)
                barrier()
            for oc in range(16):
                b0, b1 = (0, 1) if oc % 2 == 0 else (2, 3)
                wslot = (oc // 2) % 2
                if oc % 2 == 0:
                    dma("gpsimd", "WB%d" % wslot,
                        wbufs[:, 4096 * wslot:4096 * wslot + 4096]
                        .rearrange("p (i c) -> p i c", c=2048),
                        wo.tensor.ap()[oc:oc + 2].transpose([1, 0, 2]),
                        writes=(("wb", wslot),))
                wbuf = wbufs[:, 4096 * wslot + 2048 * (oc % 2):
                             4096 * wslot + 2048 * (oc % 2) + 2048]
                for dc in range(NDC):
                    mm(b0, PBS[b0][:, 0:512], wbuf[:, 128 * dc:128 * dc + 128],
                       ar(1024 * dc, 512), start=(dc == 0), stop=(dc == NDC - 1),
                       reads=(("wb", wslot), ("xcwt")))
                    mm(b1, PBS[b1][:, 0:512], wbuf[:, 128 * dc:128 * dc + 128],
                       ar(1024 * dc + 512, 512), start=(dc == 0),
                       stop=(dc == NDC - 1), reads=(("wb", wslot),))
                eng = "vector" if oc % 2 == 0 else "scalar"
                copy_ev(eng, ar(R0C + 1024 * oc, 512), PBS[b0][:, 0:512],
                        banks=(b0,), writes=(("wost", oc),))
                copy_ev(eng, ar(R0C + 1024 * oc + 512, 512), PBS[b1][:, 0:512],
                        banks=(b1,), writes=(("wost", oc),))
                dma("sync", "ss",
                    s_sp.ap()[128 * oc:128 * oc + 128,
                              HALF * half:HALF * half + 1024],
                    ar(R0C + 1024 * oc, 1024), reads=(("wost", oc),))
            barrier()
        barrier()

        for tt in range(NTOK if phases >= 6 else 0):
            par = tt % 2
            xs = XS(par)
            gt = ar(4096 * par, 2048)
            st = ar(4096 * par + 2048, 2048)
            t_x = dma("sync", "XS%d" % par, xs, x_in[128 * tt:128 * tt + 128, :],
                      writes=(("xst", par),))
            t_g = dma("sync", "CG%d" % par, gt,
                      g_sp.ap()[128 * tt:128 * tt + 128, :],
                      writes=(("cg", par),))
            t_s = dma("scalar", "CS%d" % par, st,
                      s_sp.ap()[:, 128 * tt:128 * tt + 128],
                      transpose=True, writes=(("cs", par),))
            for ch in range(4):
                tv = vop("vector", "tensor_mul", STA_C,
                         st[:, 512 * ch:512 * ch + 512],
                         gt[:, 512 * ch:512 * ch + 512],
                         reads=(("cg", par), ("cs", par)), writes=(("sta"),),
                         wait=merge(t_g, t_s))
                vop("vector", "tensor_add", xs[:, 512 * ch:512 * ch + 512],
                    xs[:, 512 * ch:512 * ch + 512], STA_C,
                    reads=(("sta"), ("xst", par)), writes=(("xst", par),),
                    wait=merge(tv, t_x))
            dma("sync", "Y%d" % par, y_out[128 * tt:128 * tt + 128, :], xs,
                reads=(("xst", par),))
        barrier()

        with nc.Block() as block:
            def run(eng_obj, key):
                for op in p.ops[key]:
                    if isinstance(op, tuple) and op[0] == "wait":
                        eng_obj.wait_ge(SEM[op[1]], op[2])
                    elif isinstance(op, tuple) and op[0] == "dma":
                        op[2](eng_obj)
                    else:
                        op(eng_obj)
            @block.tensor
            def _(t):
                run(t, "tensor")
            @block.vector
            def _(v):
                run(v, "vector")
            @block.scalar
            def _(s):
                run(s, "scalar")
            @block.gpsimd
            def _(g):
                run(g, "gpsimd")
            @block.sync
            def _(s):
                run(s, "sync")
    return nc


def _retile_proj(w):
    wt = np.ascontiguousarray(w.T.astype(np.float32))
    r = wt.reshape(16, 128, 16, 128)
    r = np.ascontiguousarray(r.transpose(2, 1, 0, 3))
    return r.reshape(16, 128, 2048)


def _retile_gu(w):
    wt = np.ascontiguousarray(w.T)
    r = wt.reshape(16, 128, 64, 128)
    r = np.ascontiguousarray(r.transpose(2, 1, 0, 3))
    return r.reshape(64, 128, 2048)


def _retile_dw(w):
    wt = np.ascontiguousarray(w.T)
    return np.ascontiguousarray(wt.reshape(64, 128, 2048))


def _consts():
    cb = np.zeros((128, 904), np.float32)
    cb[:, CB_I128:CB_I128 + 128] = np.eye(128)
    cb[:, CB_NEGI128:CB_NEGI128 + 128] = -np.eye(128)
    strict = np.zeros((128, 128), np.float32)
    diag = np.zeros((128, 128), np.float32)
    for c in range(8):
        strict[:16 * c, 16 * c:16 * c + 16] = 1.0
        diag[16 * c:16 * c + 16, 16 * c:16 * c + 16] = np.triu(
            np.ones((16, 16), np.float32))
    for rr in (0, 64):
        cb[rr:rr + 64, CB_STRICT:CB_STRICT + 128] = strict[0:64]
        cb[rr:rr + 64, CB_STRICT_HI:CB_STRICT_HI + 128] = strict[64:128]
        cb[rr:rr + 64, CB_DIAG:CB_DIAG + 128] = diag[0:64]
        cb[rr:rr + 64, CB_DIAG_HI:CB_DIAG_HI + 128] = diag[64:128]
    cb[0:1, CB_N128ROW:CB_N128ROW + 128] = -128.0
    cb[0, CB_ONE] = 1.0
    cb[:, CB_N128COL] = -128.0
    cf = np.zeros((128, 8), np.float32)
    invk = 1.0 / np.arange(1, CS + 1, dtype=np.float32)
    cf[:, CF_INVK] = np.tile(invk, 8) / HF
    cf[:, CF_NEGINVK] = -cf[:, CF_INVK]
    cf[:, CF_EPS] = 1e-5
    cf[:, CF_INVD] = 1.0 / D
    return cb.astype(bfl), cf


_prog_cache = {}


def kernel(L0_attn_hidden_states, L0_attn_Wq, L0_attn_Wk, L0_attn_Wv,
           L0_attn_Wo, L0_attn_W_ilr, L0_attn_b_ilr, L0_attn_W1,
           L0_ffn_gate_w, L0_ffn_gate_scale, L0_ffn_up_w, L0_ffn_up_scale,
           L0_ffn_down_w, L0_ffn_down_scale):
    x = np.asarray(L0_attn_hidden_states, np.float32).reshape(L, D)
    cbv, cfv = _consts()
    wilr = np.ascontiguousarray(
        np.asarray(L0_attn_W_ilr, np.float32).T.reshape(16, 128, 32)
        .transpose(1, 0, 2)).reshape(128, 512)
    w1v = np.ascontiguousarray(
        np.asarray(L0_attn_W1, np.float32).reshape(16, 128, 64)
        .transpose(1, 0, 2)).reshape(128, 1024)
    base = dict(
        wq=_retile_proj(np.asarray(L0_attn_Wq, np.float32)),
        wk=_retile_proj(np.asarray(L0_attn_Wk, np.float32)),
        wv=_retile_proj(np.asarray(L0_attn_Wv, np.float32)),
        wo=_retile_proj(np.asarray(L0_attn_Wo, np.float32)),
        wilr=wilr,
        bilr=np.asarray(L0_attn_b_ilr, np.float32).reshape(32, 1),
        w1=w1v,
        guw=np.ascontiguousarray(np.concatenate(
            [_retile_gu(np.asarray(L0_ffn_gate_w, np.int32)),
             _retile_gu(np.asarray(L0_ffn_up_w, np.int32))], axis=2)),
        dw=_retile_dw(np.asarray(L0_ffn_down_w, np.int32)),
        gs_t=np.ascontiguousarray(
            np.asarray(L0_ffn_gate_scale, np.float32).reshape(64, 128).T),
        us_t=np.ascontiguousarray(
            np.asarray(L0_ffn_up_scale, np.float32).reshape(64, 128).T),
        ds_b=np.ascontiguousarray(np.broadcast_to(
            np.asarray(L0_ffn_down_scale, np.float32)[None, :], (128, 2048))),
        cb=cbv, cf=cfv,
        i64f=np.concatenate(
            [np.concatenate([np.eye(64, dtype=np.float32)] * 2, 0),
             np.concatenate([np.eye(32, dtype=np.float32)] * 4, 0)], 1),
    )
    in_maps = []
    for core in range(NCORE):
        pm = np.zeros((128, 16), np.float32)
        for j in range(NCORE - 1):
            m = 1.0 if j < core else 0.0
            pm[:, j] = m
            pm[:, 8 + j] = 1.0 - m
        d = dict(base)
        d["x"] = np.ascontiguousarray(x[TOKC * core:TOKC * (core + 1)])
        d["pmask"] = pm
        in_maps.append(d)

    if "nc" not in _prog_cache:
        _prog_cache["nc"] = build_program(int(os.environ.get("KPHASES", "99")))
    nc = _prog_cache["nc"]
    res = run_bass_kernel_spmd(nc, in_maps, core_ids=list(range(NCORE)))
    out = np.concatenate([res.results[i]["y"] for i in range(NCORE)], axis=0)
    return out.reshape(1, L, D).astype(np.float32)


# version 16 (final)
# speedup vs baseline: 1.1031x; 1.1031x over previous
import os
import numpy as np
import ml_dtypes
from contextlib import ExitStack
import concourse.bass as bass
import concourse.mybir as mybir
from concourse.bass_utils import run_bass_kernel_spmd

F32 = mybir.dt.float32
BF16 = mybir.dt.bfloat16
I32 = mybir.dt.int32
AF = mybir.ActivationFunctionType
ALU = mybir.AluOpType
AX = mybir.AxisListType

NCORE = 8
L, D, NH, HF, CS = 16384, 2048, 32, 64, 16
TOKC = L // NCORE
TSC = 128
NSC = TOKC // TSC
HALF = TOKC // 2
SC_H = NSC // 2
NPAIR = NH // 2
NDC = D // 128
NIC = 8192 // 128
QTR = 512
NTOK = TOKC // 128
bfl = ml_dtypes.bfloat16

CB_I128, CB_NEGI128 = 0, 128
CB_STRICT, CB_STRICT_HI, CB_DIAG, CB_DIAG_HI = 256, 384, 512, 640
CB_N128ROW, CB_ONE, CB_N128COL = 768, 896, 897
CF_INVK, CF_NEGINVK, CF_EPS, CF_INVD = 0, 1, 2, 3

R0C = 16384
R1C = 32768


class Plan:
    ENGS = ("tensor", "vector", "scalar", "gpsimd", "sync")

    def __init__(self):
        self.ops = {e: [] for e in self.ENGS}
        self.cnt = {}
        self.waited = {e: {} for e in self.ENGS}
        self.guard = {}

    def inc(self, s, amt):
        v = self.cnt.get(s, 0) + amt
        self.cnt[s] = v
        return {s: v}

    def wait(self, eng, tok):
        if not tok:
            return
        for s, v in tok.items():
            if v > 0 and self.waited[eng].get(s, 0) < v:
                self.waited[eng][s] = v
                self.ops[eng].append(("wait", s, v))

    def tok(self, *sems):
        return {s: self.cnt.get(s, 0) for s in sems}

    def g(self, key):
        if key not in self.guard:
            self.guard[key] = {"w": {}, "r": {}}
        return self.guard[key]


def merge(*toks):
    out = {}
    for t in toks:
        if t:
            for s, v in t.items():
                out[s] = max(out.get(s, 0), v)
    return out


def build_program(phases=99):
    nc = bass.Bass("TRN2", target_bir_lowering=False, debug=False)

    x_in = nc.dram_tensor("x", [TOKC, D], F32, kind="ExternalInput").ap()
    wq = nc.dram_tensor("wq", [16, 128, 2048], F32, kind="ExternalInput").ap()
    wk = nc.dram_tensor("wk", [16, 128, 2048], F32, kind="ExternalInput").ap()
    wv = nc.dram_tensor("wv", [16, 128, 2048], F32, kind="ExternalInput").ap()
    wo = nc.dram_tensor("wo", [16, 128, 2048], F32, kind="ExternalInput").ap()
    wilr_d = nc.dram_tensor("wilr", [128, 512], F32, kind="ExternalInput").ap()
    bilr_d = nc.dram_tensor("bilr", [32, 1], F32, kind="ExternalInput").ap()
    w1_d = nc.dram_tensor("w1", [128, 1024], F32, kind="ExternalInput").ap()
    guw = nc.dram_tensor("guw", [64, 128, 4096], I32, kind="ExternalInput").ap()
    dw = nc.dram_tensor("dw", [64, 128, 2048], I32, kind="ExternalInput").ap()
    gs_d = nc.dram_tensor("gs_t", [128, 64], F32, kind="ExternalInput").ap()
    us_d = nc.dram_tensor("us_t", [128, 64], F32, kind="ExternalInput").ap()
    ds_d = nc.dram_tensor("ds_b", [128, 2048], F32, kind="ExternalInput").ap()
    cb_d = nc.dram_tensor("cb", [128, 904], BF16, kind="ExternalInput").ap()
    cf_d = nc.dram_tensor("cf", [128, 8], F32, kind="ExternalInput").ap()
    i64_d = nc.dram_tensor("i64f", [128, 96], F32, kind="ExternalInput").ap()
    pm_d = nc.dram_tensor("pmask", [128, 16], F32, kind="ExternalInput").ap()
    y_out = nc.dram_tensor("y", [TOKC, D], F32, kind="ExternalOutput").ap()

    xn_sp = nc.dram_tensor("xn_sp", [TOKC, D], BF16)
    g_sp = nc.dram_tensor("g_sp", [TOKC, D], BF16)
    xb_sp = nc.dram_tensor("xb_sp", [2, 128, 16 * HALF], BF16)
    xa_sp = nc.dram_tensor("xa_sp", [2, 128, 16 * HALF], BF16)
    s_sp = nc.dram_tensor("s_sp", [D, TOKC], BF16)
    cc_in = nc.dram_tensor("cc_in", [128, NPAIR * 128], F32)
    cc_out = nc.dram_tensor("cc_out", [NCORE * 128, NPAIR * 128], F32)

    p = Plan()

    es = ExitStack()
    with es:
        cbS = es.enter_context(nc.sbuf_tensor("consts_b", [128, 904], BF16))
        cfS = es.enter_context(nc.sbuf_tensor("consts_f", [128, 8], F32))
        idf = es.enter_context(nc.sbuf_tensor("ident64f", [128, 96], F32))
        pmS = es.enter_context(nc.sbuf_tensor("pmasks", [128, 16], F32))
        wilrS = es.enter_context(nc.sbuf_tensor("wilr_sb", [128, 512], BF16))
        bilrS = es.enter_context(nc.sbuf_tensor("bilr_sb", [32, 1], F32))
        S_all = es.enter_context(nc.sbuf_tensor("S_all", [128, NPAIR * 64], F32))
        Arun = es.enter_context(nc.sbuf_tensor("Arun", [128, NPAIR * 64], F32))
        Brun = es.enter_context(nc.sbuf_tensor("Brun", [128, NPAIR * 64], F32))
        ilrT = es.enter_context(nc.sbuf_tensor("ilrT", [128, 512], F32))
        rsrow = es.enter_context(nc.sbuf_tensor("rs_row", [1, TOKC], BF16))
        gscol = es.enter_context(nc.sbuf_tensor("gscol", [128, 64], F32))
        uscol = es.enter_context(nc.sbuf_tensor("uscol", [128, 64], F32))

        arena = es.enter_context(nc.sbuf_tensor("arena", [128, R0C + R1C], BF16))
        wbufs = es.enter_context(nc.sbuf_tensor("wbufs", [128, 2 * 4096], BF16))
        wdbuf = es.enter_context(nc.sbuf_tensor("wdbuf", [128, 2 * 2048], BF16))
        xstage = es.enter_context(nc.sbuf_tensor("xstage", [128, 2 * 2048], F32))
        stage_a = es.enter_context(nc.sbuf_tensor("stage_a", [128, 512], F32))
        ccbv = es.enter_context(nc.sbuf_tensor("ccbuf", [128, 2048], F32))
        colbuf = es.enter_context(nc.sbuf_tensor("colbuf", [128, 64], F32))
        hscorr = es.enter_context(nc.sbuf_tensor("hscorr", [128, 4], BF16))
        hsrow = es.enter_context(nc.sbuf_tensor("hsrow", [1, 512], BF16))
        rscolb = es.enter_context(nc.sbuf_tensor("rscol_b", [128, 1], BF16))
        MTs = es.enter_context(nc.sbuf_tensor("MTs", [128, 2 * 512], BF16))
        ATs_sb = es.enter_context(nc.sbuf_tensor("ATsb", [128, 2 * 512], BF16))
        ATd_sb = es.enter_context(nc.sbuf_tensor("ATdb", [128, 2 * 512], BF16))
        Z1lo = es.enter_context(nc.sbuf_tensor("Z1lo", [128, 2 * 256], BF16))
        Z1hi = es.enter_context(nc.sbuf_tensor("Z1hi", [128, 2 * 256], BF16))
        XBcs = es.enter_context(nc.sbuf_tensor("XBcs", [128, 2 * 256], BF16))
        SbdT = es.enter_context(nc.sbuf_tensor("Sbd", [128, 4 * 128], BF16))
        OUTsb = es.enter_context(nc.sbuf_tensor("OUTsb", [128, 2 * 128], BF16))
        cotokf = es.enter_context(nc.sbuf_tensor("cotokf", [128, 2 * 32], F32))
        ncotokf = es.enter_context(nc.sbuf_tensor("ncotokf", [128, 2 * 32], F32))
        cothi = es.enter_context(nc.sbuf_tensor("cothi", [64, 2 * 32], F32))
        ncothi = es.enter_context(nc.sbuf_tensor("ncothi", [64, 2 * 32], F32))
        colo64 = es.enter_context(nc.sbuf_tensor("colo64", [128, 2 * 32], F32))
        ncolo64 = es.enter_context(nc.sbuf_tensor("ncolo64", [128, 2 * 32], F32))
        dTs = es.enter_context(nc.sbuf_tensor("dTs", [128, 4 * 64], F32))
        ttmp = es.enter_context(nc.sbuf_tensor("ttmp", [128, 4 * 64], F32))
        odst = es.enter_context(nc.sbuf_tensor("odst", [128, 2 * 128], F32))
        pb0 = es.enter_context(nc.psum_tensor("pb0", [128, 512], F32))
        pb1 = es.enter_context(nc.psum_tensor("pb1", [128, 512], F32))
        pb2 = es.enter_context(nc.psum_tensor("pb2", [128, 512], F32))
        pb3 = es.enter_context(nc.psum_tensor("pb3", [128, 512], F32))
        pb4 = es.enter_context(nc.psum_tensor("pb4", [128, 512], F32))
        pb5 = es.enter_context(nc.psum_tensor("pb5", [128, 512], F32))
        pb6 = es.enter_context(nc.psum_tensor("pb6", [128, 512], F32))
        pb7 = es.enter_context(nc.psum_tensor("pb7", [128, 512], F32))
        SG = es.enter_context(nc.semaphore("sg"))
        SS = es.enter_context(nc.semaphore("ss"))
        SX = es.enter_context(nc.semaphore("sx"))
        PESEM = es.enter_context(nc.semaphore("pe"))
        DVSEM = es.enter_context(nc.semaphore("dv"))
        ACSEM = es.enter_context(nc.semaphore("ac"))
        GCSEM = es.enter_context(nc.semaphore("gc"))
        CCSEM = es.enter_context(nc.semaphore("cc"))
        SEM = {"sg": SG, "ss": SS, "sx": SX, "pe": PESEM, "dv": DVSEM,
               "ac": ACSEM, "gc": GCSEM, "cc": CCSEM}
        for _nm in ("WB0", "WB1", "WB2", "WD0", "WD1", "XS0", "XS1",
                    "SP0", "SP1", "SP2", "SP3", "CG0", "CG1", "CS0", "CS1",
                    "GS0", "GS1", "GS2", "GS3", "CCA", "CCB", "CCIN",
                    "SPB", "SPA", "RLD", "Y0", "Y1"):
            SEM[_nm] = es.enter_context(nc.semaphore(_nm))
        PBS = [pb0, pb1, pb2, pb3, pb4, pb5, pb6, pb7]

        def dma(eng, sem, out, in_, transpose=False, reads=(), writes=()):
            for k in writes:
                p.wait(eng, merge(p.g(k)["r"], p.g(k)["w"]))
            for k in reads:
                p.wait(eng, p.g(k)["w"])
            t = p.inc(sem, 16)
            def fn(e, out=out, in_=in_, tp=transpose, sem=sem):
                e.dma_start(out=out, in_=in_, transpose=tp).then_inc(SEM[sem], 16)
            p.ops[eng].append(("dma", sem, fn))
            for k in writes:
                p.g(k)["w"] = merge(p.g(k)["w"], t)
            for k in reads:
                p.g(k)["r"] = merge(p.g(k)["r"], t)
            return t

        bank_writes = {b: [] for b in range(8)}
        PSUM_IDX = {"pb%d" % i: i for i in range(8)}

        def _prange(apx):
            st, cnt = apx.ap[0]
            return (apx.offset // st, apx.offset // st + cnt)

        def _halves(pr_):
            out = []
            if pr_[0] < 64:
                out.append(0)
            if pr_[1] > 64:
                out.append(1)
            return out

        def _psum_aps(args):
            found = []
            for a in args:
                try:
                    nm = a.tensor.name
                except Exception:
                    continue
                if nm in PSUM_IDX:
                    found.append((PSUM_IDX[nm], _prange(a)))
            return found

        def mm(bank, out, lhsT, rhs, start=True, stop=True, tp=None,
               reads=(), wait=None):
            ohs = _halves(_prange(out))
            for h_ in ohs:
                p.wait("tensor", p.g(("pb", bank, h_))["r"])
            for k in reads:
                p.wait("tensor", p.g(k)["w"])
            if wait:
                p.wait("tensor", wait)
            rows, outp = _prange(lhsT), _prange(out)
            waited_pe = p.waited["tensor"].get("pe", 0)
            live = []
            for (r2, o2, t2) in bank_writes[bank]:
                if t2 <= waited_pe:
                    continue
                if (rows[1] <= r2[0] or r2[1] <= rows[0]) and \
                        not (outp[1] <= o2[0] or o2[1] <= outp[0]):
                    p.wait("tensor", {"pe": t2})
                    waited_pe = max(waited_pe, t2)
                else:
                    live.append((r2, o2, t2))
            bank_writes[bank] = [e for e in live if e[2] > waited_pe]
            t = p.inc("pe", 1)
            def fn(e, out=out, lhsT=lhsT, rhs=rhs, start=start, stop=stop, tp=tp):
                e.matmul(out, lhsT, rhs, start=start, stop=stop,
                         skip_group_check=True, tile_position=tp).then_inc(
                             SEM["pe"], 1)
            p.ops["tensor"].append(fn)
            bank_writes[bank].append((rows, outp, t["pe"]))
            for h_ in ohs:
                g2 = p.g(("pb", bank, h_))
                g2["w"] = merge(g2["w"], t)
            for k in reads:
                p.g(k)["r"] = merge(p.g(k)["r"], t)
            return t

        ENG_SEM = {"vector": "dv", "scalar": "ac", "gpsimd": "gc"}

        def vop(eng, meth, *args, banks=(), reads=(), writes=(), wait=None,
                **kwargs):
            pa = _psum_aps(list(args) + list(kwargs.values()))
            for b, pr_ in pa:
                for h_ in _halves(pr_):
                    p.wait(eng, p.g(("pb", b, h_))["w"])
            for k in reads:
                p.wait(eng, p.g(k)["w"])
            for k in writes:
                p.wait(eng, merge(p.g(k)["r"], p.g(k)["w"]))
            if wait:
                p.wait(eng, wait)
            s = ENG_SEM[eng]
            t = p.inc(s, 1)
            def fn(e, meth=meth, args=args, kwargs=kwargs, s=s):
                getattr(e, meth)(*args, **kwargs).then_inc(SEM[s], 1)
            p.ops[eng].append(fn)
            for b, pr_ in pa:
                for h_ in _halves(pr_):
                    g2 = p.g(("pb", b, h_))
                    g2["r"] = merge(g2["r"], t)
                bank_writes[b] = [e for e in bank_writes[b]
                                  if e[1][1] <= pr_[0] or pr_[1] <= e[1][0]]
            for k in reads:
                p.g(k)["r"] = merge(p.g(k)["r"], t)
            for k in writes:
                p.g(k)["w"] = merge(p.g(k)["w"], t)
            return t

        def copy_ev(eng, out, in_, banks=(), reads=(), writes=(), wait=None):
            meth = "tensor_copy" if eng in ("vector", "gpsimd") else "copy"
            return vop(eng, meth, out, in_, banks=banks, reads=reads,
                       writes=writes, wait=wait)

        def barrier():
            t = p.tok(*SEM.keys())
            for e in Plan.ENGS:
                p.wait(e, t)
            return t

        def ar(off, cols):
            return arena[:, off:off + cols]

        XS = lambda par: xstage[:, 2048 * par:2048 * par + 2048]
        STA_F = stage_a[:, 0:512]
        STA_C = stage_a[:, 0:512]
        CCB = ccbv

        t0 = [
            dma("gpsimd", "sg", cbS[:], cb_d),
            dma("gpsimd", "sg", cfS[:], cf_d),
            dma("gpsimd", "sg", idf[:], i64_d),
            dma("gpsimd", "sg", pmS[:], pm_d),
            dma("gpsimd", "sg", wilrS[:], wilr_d),
            dma("gpsimd", "sg", bilrS[:], bilr_d),
            dma("gpsimd", "sg", S_all[:], w1_d, writes=(("sall"),)),
            dma("gpsimd", "sg", gscol[:], gs_d),
            dma("gpsimd", "sg", uscol[:], us_d),
        ]
        t_consts = merge(*t0)
        p.wait("vector", t_consts)
        for pr in range(NPAIR):
            for hh in range(2):
                copy_ev("vector", Arun[64 * hh:64 * hh + 64, 64 * pr:64 * pr + 64],
                        idf[64 * hh:64 * hh + 64, 0:64], writes=(("arun"),))
        vop("vector", "memset", Brun[:], 0.0, writes=(("brun"),))
        vop("vector", "memset", SbdT[:], 0.0, writes=(("sbd", 0), ("sbd", 1)))

        for tt in range(NTOK if phases >= 1 else 0):
            par = tt % 2
            slot = tt % 4
            xs = XS(par)
            xn_t = ar(2048 * slot, 2048)
            t_x = dma("sync", "XS%d" % par, xs, x_in[128 * tt:128 * tt + 128, :],
                      writes=(("xst", par),))
            PB = 16 * par
            p.wait("vector", t_x)
            tst = None
            for c in range(4):
                tst = vop("vector", "bn_stats", colbuf[:, 32 + 6 * c:38 + 6 * c],
                          xs[:, 512 * c:512 * c + 512], reads=(("xst", par),))
            tv = vop("vector", "bn_aggr", colbuf[:, PB:PB + 2],
                     colbuf[:, 32:56], writes=(("cb", par),), wait=tst)
            ta = vop("scalar", "activation", colbuf[:, PB + 3:PB + 4],
                     colbuf[:, PB + 1:PB + 2], AF.Sqrt,
                     bias=cfS[:, CF_EPS:CF_EPS + 1], wait=tv,
                     reads=(("cb", par),), writes=(("cb", par),))
            ta = vop("vector", "reciprocal", colbuf[:, PB + 2:PB + 3],
                     colbuf[:, PB + 3:PB + 4], wait=ta,
                     reads=(("cb", par),), writes=(("cb", par),))
            tn = vop("vector", "tensor_scalar", xn_t, xs,
                     colbuf[:, PB:PB + 1], colbuf[:, PB + 2:PB + 3],
                     ALU.subtract, ALU.mult,
                     reads=(("xst", par), ("cb", par)), writes=(("r0", slot),),
                     wait=ta)
            tr = vop("vector", "tensor_reduce", colbuf[:, PB + 4:PB + 5], xn_t,
                     AX.X, ALU.add, reads=(("r0", slot),),
                     writes=(("cb", par),))
            trb = copy_ev("vector", rscolb[:], colbuf[:, PB + 4:PB + 5],
                          wait=tr, reads=(("cb", par),), writes=(("rscb"),))
            mm(5, pb5[0:1, 0:128], rscolb[:], cbS[:, CB_I128:CB_I128 + 128],
               reads=(("rscb"),))
            copy_ev("vector", rsrow[0:1, 128 * tt:128 * tt + 128],
                    pb5[0:1, 0:128], banks=(5,), writes=(("rsrow"),))
            t_sp = dma("sync", "SP%d" % slot,
                       xn_sp.ap()[128 * tt:128 * tt + 128, :],
                       xn_t, reads=(("r0", slot),))
            p.g(("xnsp", tt))["w"] = t_sp
        barrier()

        def build_xnT(tok0, ntok, tile_cols, dst_base=0):
            toks = []
            sp_keys = tuple(("xnsp", tt) for tt in
                            range(tok0 // 128, (tok0 + ntok) // 128))
            for dc in range(NDC):
                toks.append(dma("scalar", "sx",
                                ar(dst_base + tile_cols * dc, ntok),
                                xn_sp.ap()[tok0:tok0 + ntok, 128 * dc:128 * dc + 128],
                                transpose=True, writes=(("xnq"),),
                                reads=sp_keys))
            return merge(*toks)

        HT_O = R0C
        DSB_O = 11264
        if phases >= 2:
            dma("gpsimd", "sg", ar(DSB_O, 2048), ds_d, writes=(("dsba"),))
        for qq in range(4 if phases >= 2 else 0):
            build_xnT(QTR * qq, QTR, QTR)
            for ic in range(NIC):
                slot = ic % 2
                bg, bu = (6, 7) if ic % 2 == 0 else (4, 5)
                dma("gpsimd", "WB%d" % slot,
                    wbufs[:, 4096 * slot:4096 * slot + 4096], guw[ic],
                    writes=(("wb", slot),))
                gbuf = wbufs[:, 4096 * slot:4096 * slot + 2048]
                ubuf = wbufs[:, 4096 * slot + 2048:4096 * slot + 4096]
                for dc in range(NDC):
                    mm(bg, PBS[bg][:, 0:QTR], gbuf[:, 128 * dc:128 * dc + 128],
                       ar(512 * dc, QTR), start=(dc == 0), stop=False,
                       reads=(("wb", slot), ("xnq")) if dc == 0 else (("wb", slot),))
                for dc in range(NDC):
                    mm(bu, PBS[bu][:, 0:QTR], ubuf[:, 128 * dc:128 * dc + 128],
                       ar(512 * dc, QTR), start=(dc == 0), stop=False,
                       reads=(("wb", slot),))
                rs = rsrow[0:1, QTR * qq:QTR * qq + QTR]
                mm(bg, PBS[bg][:, 0:QTR], cbS[0:1, CB_N128ROW:CB_N128ROW + 128],
                   rs, start=False, stop=True)
                mm(bu, PBS[bu][:, 0:QTR], cbS[0:1, CB_N128ROW:CB_N128ROW + 128],
                   rs, start=False, stop=True)
                silu_st = ar(8192 + 512 * (ic % 2), QTR)
                ta = vop("scalar", "activation", silu_st, PBS[bg][:, 0:QTR],
                         AF.Silu, scale=gscol[:, ic:ic + 1], banks=(bg,),
                         writes=(("silu", ic % 2),))
                h_t = ar(HT_O + 512 * ic, QTR)
                vop("vector", "scalar_tensor_tensor", h_t, PBS[bu][:, 0:QTR],
                    uscol[:, ic:ic + 1], silu_st, ALU.mult, ALU.mult,
                    banks=(bu,), reads=(("silu", ic % 2),), writes=(("hT"),),
                    wait=ta)
            for ic in range(NIC):
                mm(0, pb0[0:1, 0:QTR], cbS[:, CB_N128COL:CB_N128COL + 1],
                   ar(HT_O + 512 * ic, QTR), start=(ic == 0),
                   stop=(ic == NIC - 1), reads=(("hT"),))
            th = copy_ev("vector", hsrow[0:1, 0:QTR], pb0[0:1, 0:QTR], banks=(0,),
                         writes=(("hsr"),))
            for k in range(4):
                mm(1, pb1[0:128, 4 + k:5 + k], hsrow[0:1, 128 * k:128 * k + 128],
                   cbS[0:1, CB_ONE:CB_ONE + 1], start=True,
                   reads=(("hsr"),))
            copy_ev("vector", hscorr[:, 0:4], pb1[0:128, 4:8], banks=(1,),
                    writes=(("hsc"),))
            for dq in range(4):
                for ic in range(NIC):
                    icg = ic // 4
                    igo = ic % 4
                    wslot = icg % 2
                    if igo == 0:
                        dma("gpsimd", "WD%d" % wslot,
                            wdbuf[:, 2048 * wslot:2048 * wslot + 2048]
                            .rearrange("p (i c) -> p i c", c=512),
                            dw[4 * icg:4 * icg + 4, :, 512 * dq:512 * dq + 512]
                            .transpose([1, 0, 2]),
                            writes=(("wd", wslot),))
                    wbd = wdbuf[:, 2048 * wslot + 512 * igo:
                                2048 * wslot + 512 * igo + 512]
                    for tk in range(4):
                        mm(tk, PBS[tk][:, 0:512],
                           ar(HT_O + 512 * ic + 128 * tk, 128), wbd,
                           start=(ic == 0), stop=(ic == NIC - 1),
                           reads=(("wd", wslot), ("hT")) if tk == 0 else (("wd", wslot),))
                for tk in range(4):
                    tv = vop("vector", "scalar_tensor_tensor", STA_F,
                             PBS[tk][:, 0:512], hscorr[:, tk:tk + 1],
                             ar(DSB_O + 512 * dq, 512), ALU.add, ALU.mult,
                             banks=(tk,), reads=(("hsc"), ("dsba")),
                             writes=(("sta"),))
                    g_st = ar(9216 + 512 * tk, 512)
                    ta = vop("scalar", "activation", g_st, STA_F,
                             AF.Silu, reads=(("sta"),), writes=(("gst", tk),),
                             wait=tv)
                    dma("sync", "GS%d" % tk,
                        g_sp.ap()[QTR * qq + 128 * tk:QTR * qq + 128 * tk + 128,
                                  512 * dq:512 * dq + 512], g_st,
                        reads=(("gst", tk),))
        barrier()


        def project(wap, dst_off, xn_base=0, n_tok=HALF, wap_base=0):
            ncol = n_tok // 512
            for oc in range(16):
                b0, b1 = (0, 1) if oc % 2 == 0 else (2, 3)
                wslot = (oc // 2) % 2
                if oc % 2 == 0:
                    dma("gpsimd", "WB%d" % wslot,
                        wbufs[:, 4096 * wslot:4096 * wslot + 4096]
                        .rearrange("p (i c) -> p i c", c=2048),
                        wap.tensor.ap()[wap_base + oc:wap_base + oc + 2]
                        .transpose([1, 0, 2]),
                        writes=(("wb", wslot),))
                wbuf = wbufs[:, 4096 * wslot + 2048 * (oc % 2):
                             4096 * wslot + 2048 * (oc % 2) + 2048]
                for cc2 in range(ncol):
                    bnk = (b0, b1)[cc2 % 2]
                    for dc in range(NDC):
                        mm(bnk, PBS[bnk][:, 0:512],
                           wbuf[:, 128 * dc:128 * dc + 128],
                           ar(xn_base + n_tok * dc + 512 * cc2, 512),
                           start=(dc == 0), stop=(dc == NDC - 1),
                           reads=(("wb", wslot), ("xnq")))
                    eng = "vector" if (oc + cc2) % 2 == 0 else "scalar"
                    copy_ev(eng, ar(dst_off + n_tok * oc + 512 * cc2, 512),
                            PBS[bnk][:, 0:512], banks=(bnk,),
                            writes=(("proj"),))

        def ilr_proj(half):
            for c2 in range(2):
                b = 2 * half + c2
                for dc in range(NDC):
                    mm(4, pb4[0:32, 0:512], wilrS[:, 32 * dc:32 * dc + 32],
                       ar(1024 * dc + 512 * c2, 512), start=(dc == 0),
                       stop=(dc == NDC - 1), reads=(("xnq"),))
                vop("scalar", "activation",
                    ilrT[32 * b:32 * b + 32, 0:512],
                    pb4[0:32, 0:512], AF.Sigmoid, bias=bilrS[:, 0:1], banks=(4,),
                    writes=(("ilrt"),))

        def co_build(half, s):
            sl = s % 2
            base = HALF * half + TSC * s
            b, boff = base // 512, base % 512
            tpb = (32 * b, 0) if b == 3 else None
            mm(6, pb6[0:128, 448:480],
               ilrT[32 * b:32 * b + 32, boff:boff + 128],
               idf[32 * b:32 * b + 32, 64:96], reads=(("ilrt"),), tp=tpb)
            mm(6, pb6[0:64, 480:512],
               ilrT[32 * b:32 * b + 32, boff + 64:boff + 128],
               idf[32 * b:32 * b + 32, 64:96], start=True, tp=tpb)
            vop("vector", "tensor_scalar", cotokf[:, 32 * sl:32 * sl + 32],
                pb6[0:128, 448:480], cfS[:, CF_INVK:CF_INVK + 1],
                None, ALU.mult, banks=(6,), writes=(("co", sl),))
            vop("vector", "tensor_scalar", ncotokf[:, 32 * sl:32 * sl + 32],
                pb6[0:128, 448:480],
                cfS[:, CF_NEGINVK:CF_NEGINVK + 1], None, ALU.mult, banks=(6,),
                writes=(("co", sl),))
            vop("vector", "tensor_scalar", cothi[:, 32 * sl:32 * sl + 32],
                pb6[0:64, 480:512],
                cfS[0:64, CF_INVK:CF_INVK + 1], None, ALU.mult, banks=(6,),
                writes=(("co", sl),))
            vop("vector", "tensor_scalar", ncothi[:, 32 * sl:32 * sl + 32],
                pb6[0:64, 480:512],
                cfS[0:64, CF_NEGINVK:CF_NEGINVK + 1], None, ALU.mult, banks=(6,),
                writes=(("co", sl),))
            mm(6, pb6[64:128, 480:512],
               ilrT[32 * b:32 * b + 32, boff:boff + 64],
               idf[32 * b:32 * b + 32, 64:96], start=True, tp=(32 * b, 64))
            vop("vector", "tensor_scalar", colo64[64:128, 32 * sl:32 * sl + 32],
                pb6[64:128, 480:512], cfS[64:128, CF_INVK:CF_INVK + 1], None,
                ALU.mult, writes=(("co", sl),))
            vop("vector", "tensor_scalar", ncolo64[64:128, 32 * sl:32 * sl + 32],
                pb6[64:128, 480:512], cfS[64:128, CF_NEGINVK:CF_NEGINVK + 1],
                None, ALU.mult, writes=(("co", sl),))

        SUB_STEPS = [
            (0, 0, 16, 0, 0, 32, 0, 0, 32),
            (0, 0, 32, 0, 32, 64, 0, 32, 64),
            (0, 0, 32, 1, 0, 64, 0, 64, 128),
            (0, 32, 48, 0, 32, 64, 0, 32, 64),
            (0, 32, 64, 1, 0, 64, 0, 64, 128),
            (1, 0, 16, 1, 0, 32, 1, 64, 96),
            (1, 0, 32, 1, 32, 64, 1, 96, 128),
            (1, 32, 48, 1, 32, 64, 1, 96, 128),
        ]

        KSP = int(os.environ.get("KSP", "9"))

        def scan_pair_gen(pr, s, half, phase):
            par = pr % 2
            rh = (pr // 2) % 2
            RB = 64 * rh
            sl = s % 2
            slot4 = 2 * par + rh
            bP_lo = bP_hi = slot4
            bGA = 4 if par == 0 else 5
            bT = 6 if par == 0 else 7
            if phase == "A":
                sc0 = TSC * s
                xbt = lambda c0, c1: ar(R0C + 1024 * pr + sc0 + c0, c1 - c0)
                xat = lambda c0, c1: ar(R0C + 16384 + 1024 * pr + sc0 + c0,
                                        c1 - c0)
                xct = None
            else:
                sc0 = TSC * (s % 4)
                xbt = lambda c0, c1: ar(R0C + 512 * pr + sc0 + c0, c1 - c0)
                xat = lambda c0, c1: ar(R0C + 8192 + 512 * pr + sc0 + c0,
                                        c1 - c0)
                xct = lambda c0, c1: ar(R0C + 16384 + 512 * pr + sc0 + c0,
                                        c1 - c0)
            hb = 128 if phase == "A" else 64
            pw = 2 * hb
            P_lo = PBS[bP_lo][RB:RB + 64, 0:pw]
            P_hi = PBS[bP_hi][RB:RB + 64, pw:2 * pw]
            zrow = cbS[32 + RB:33 + RB, CB_N128ROW:CB_N128ROW + 128]

            def zfill(bank, out_ap, ncols):
                ob = _prange(out_ap)[0]
                mm(bank, out_ap, zrow[:, 0:out_ap.shape[0]],
                   cbS[32 + RB:33 + RB, 0:ncols], start=True, stop=False,
                   tp=(32 + RB, ob))
            mts = lambda tl, hh: MTs[RB:RB + 64,
                                     512 * par + 128 * (2 * tl + hh):
                                     512 * par + 128 * (2 * tl + hh) + 128]
            z1 = lambda tl: (Z1lo if tl == 0 else Z1hi)[RB:RB + 64,
                                                        256 * par:256 * par + pw]
            cx = 32 * sl + 2 * pr + hh if False else None
            col = lambda hh: 32 * sl + 2 * pr + hh
            if rh == 0:
                co_lo = lambda hh: cotokf[0:64, col(hh):col(hh) + 1]
                nco_lo = lambda hh: ncotokf[0:64, col(hh):col(hh) + 1]
                co_hi2 = lambda hh: cothi[0:64, col(hh):col(hh) + 1]
                nco_hi2 = lambda hh: ncothi[0:64, col(hh):col(hh) + 1]
            else:
                co_lo = lambda hh: colo64[64:128, col(hh):col(hh) + 1]
                nco_lo = lambda hh: ncolo64[64:128, col(hh):col(hh) + 1]
                co_hi2 = lambda hh: cotokf[64:128, col(hh):col(hh) + 1]
                nco_hi2 = lambda hh: ncotokf[64:128, col(hh):col(hh) + 1]
            nco_full = lambda hh: ncotokf[:, col(hh):col(hh) + 1]
            msk_s = cbS[RB:RB + 64, CB_STRICT:CB_STRICT + 128]
            msk_sh = cbS[RB:RB + 64, CB_STRICT_HI:CB_STRICT_HI + 128]
            msk_d = cbS[RB:RB + 64, CB_DIAG:CB_DIAG + 128]
            msk_dh = cbS[RB:RB + 64, CB_DIAG_HI:CB_DIAG_HI + 128]
            K_MT = ("mts", par, rh)
            K_Z1 = ("z1", par, rh)
            K_XBC = ("xbc", par, rh)
            K_ATS = ("ats", par, rh)
            K_OSB = ("osb", par)
            K_SBD = ("sbd", slot4)

            for hh in range(2):
                r0 = 64 * hh
                mm(bGA, PBS[bGA][RB:RB + 64, 128 * hh:128 * hh + 128],
                   xbt(0, 64)[r0:r0 + 64, :], xbt(0, 128)[r0:r0 + 64, :],
                   start=True, reads=(("proj"),), tp=(r0, RB))
                mm(bGA, PBS[bGA][RB:RB + 64, 256 + 128 * hh:256 + 128 * hh + 128],
                   xbt(64, 128)[r0:r0 + 64, :], xbt(0, 128)[r0:r0 + 64, :],
                   start=True, tp=(r0, RB))
            for hh in range(2):
                vop("vector", "scalar_tensor_tensor", mts(0, hh),
                    PBS[bGA][RB:RB + 64, 128 * hh:128 * hh + 128], nco_lo(hh),
                    msk_s, ALU.mult, ALU.mult, reads=(("co", sl),),
                    writes=(K_MT,))
                vop("vector", "scalar_tensor_tensor", mts(1, hh),
                    PBS[bGA][RB:RB + 64, 256 + 128 * hh:256 + 128 * hh + 128],
                    nco_hi2(hh), msk_sh, ALU.mult, ALU.mult, writes=(K_MT,))

            yield
            if KSP < 2:
                return
            at_s = lambda tl, hh: ATs_sb[RB:RB + 64,
                                         512 * par + 128 * (2 * tl + hh):
                                         512 * par + 128 * (2 * tl + hh) + 128]
            at_d = lambda tl, hh: ATd_sb[RB:RB + 64,
                                         512 * par + 128 * (2 * tl + hh):
                                         512 * par + 128 * (2 * tl + hh) + 128]
            if phase == "B":
                for hh in range(2):
                    r0 = 64 * hh
                    mm(bGA, PBS[bGA][RB:RB + 64, 128 * hh:128 * hh + 128],
                       xbt(0, 64)[r0:r0 + 64, :], xct(0, 128)[r0:r0 + 64, :],
                       start=True, tp=(r0, RB))
                    mm(bGA, PBS[bGA][RB:RB + 64, 256 + 128 * hh:256 + 128 * hh + 128],
                       xbt(64, 128)[r0:r0 + 64, :], xct(0, 128)[r0:r0 + 64, :],
                       start=True, tp=(r0, RB))
                for hh in range(2):
                    vop("vector", "scalar_tensor_tensor", at_s(0, hh),
                        PBS[bGA][RB:RB + 64, 128 * hh:128 * hh + 128],
                        nco_lo(hh), msk_s, ALU.mult, ALU.mult, writes=(K_ATS,))
                    vop("vector", "tensor_tensor", at_d(0, hh),
                        PBS[bGA][RB:RB + 64, 128 * hh:128 * hh + 128],
                        msk_d, ALU.mult, writes=(K_ATS,))
                    vop("vector", "scalar_tensor_tensor", at_s(1, hh),
                        PBS[bGA][RB:RB + 64, 256 + 128 * hh:256 + 128 * hh + 128],
                        nco_hi2(hh), msk_sh, ALU.mult, ALU.mult, writes=(K_ATS,))
                    vop("vector", "tensor_tensor", at_d(1, hh),
                        PBS[bGA][RB:RB + 64, 256 + 128 * hh:256 + 128 * hh + 128],
                        msk_dh, ALU.mult, writes=(K_ATS,))
                sbd = SbdT[:, 128 * slot4:128 * slot4 + 128]
                copy_ev("vector", sbd[0:64, 0:64],
                        S_all[0:64, 64 * pr:64 * pr + 64],
                        reads=(("sall"),), writes=(K_SBD,))
                copy_ev("vector", sbd[64:128, 64:128],
                        S_all[64:128, 64 * pr:64 * pr + 64],
                        reads=(("sall"),), writes=(K_SBD,))
                zfill(bP_lo, PBS[bP_lo][RB:RB + 64, 0:2 * pw], 2 * pw)
                mm(bP_lo, P_lo, xbt(0, 64), sbd, start=False, stop=False,
                   reads=(K_SBD,), tp=(0, RB))
                mm(bP_lo, P_lo, xat(0, 64),
                   cbS[:, CB_NEGI128:CB_NEGI128 + 128], start=False, stop=False,
                   tp=(0, RB))
                mm(bP_hi, P_hi, xbt(64, 128), sbd, start=False, stop=False,
                   reads=(K_SBD,), tp=(0, RB))
                mm(bP_hi, P_hi, xat(64, 128),
                   cbS[:, CB_NEGI128:CB_NEGI128 + 128], start=False, stop=False,
                   tp=(0, RB))
            else:
                zfill(bP_lo, PBS[bP_lo][RB:RB + 64, 0:2 * pw], 2 * pw)
                for hh in range(2):
                    r0 = 64 * hh
                    i64b = cbS[r0:r0 + 64, CB_I128 + r0:CB_I128 + r0 + 64]
                    mm(bP_lo, P_lo[:, 128 * hh:128 * hh + 64],
                       xbt(0, 64)[r0:r0 + 64, :], i64b, start=False,
                       stop=False, reads=(("proj"),), tp=(r0, RB))
                    mm(bP_lo, P_lo[:, 128 * hh + 64:128 * hh + 128],
                       xat(0, 64)[r0:r0 + 64, :], i64b, start=False, stop=False,
                       tp=(r0, RB))
                    mm(bP_hi, P_hi[:, 128 * hh:128 * hh + 64],
                       xbt(64, 128)[r0:r0 + 64, :], i64b, start=False,
                       stop=False, tp=(r0, RB))
                    mm(bP_hi, P_hi[:, 128 * hh + 64:128 * hh + 128],
                       xat(64, 128)[r0:r0 + 64, :], i64b, start=False,
                       stop=False, tp=(r0, RB))

            yield
            if KSP < 3:
                return
            ei = [0]

            def evac32(tl, r0):
                src = (P_lo if tl == 0 else P_hi)[r0:r0 + 32, :]
                dst = z1(tl)[r0:r0 + 32, :]
                eng = "scalar" if (ei[0] + slot4) % 2 == 0 else "vector"
                ei[0] += 1
                return copy_ev(eng, dst, src, writes=(K_Z1,))

            def sub_step(i):
                tile_s, s0, s1, tile_d, d0, d1, mt_t, m0, m1 = SUB_STEPS[i]
                dstP = P_lo if tile_d == 0 else P_hi
                bank_d = bP_lo if tile_d == 0 else bP_hi
                for hh in range(2):
                    mm(bank_d, dstP[d0:d1, hb * hh:hb * hh + hb],
                       mts(mt_t, hh)[s0:s1, m0:m1],
                       z1(tile_s)[s0:s1, hb * hh:hb * hh + hb],
                       start=False, stop=False, reads=(K_MT, K_Z1),
                       tp=(RB + s0, RB + d0))
            evac32(0, 0)
            yield
            sub_step(0)
            evac32(0, 0)
            yield
            sub_step(1)
            sub_step(2)
            evac32(0, 32)
            yield
            sub_step(3)
            evac32(0, 32)
            yield
            sub_step(4)
            evac32(1, 0)
            yield
            sub_step(5)
            evac32(1, 0)
            yield
            sub_step(6)
            evac32(1, 32)
            yield
            sub_step(7)
            evac32(1, 32)
            yield

            if KSP < 4:
                return
            mm(bGA, PBS[bGA][RB:RB + 64, 0:128], xbt(0, 64),
               cbS[:, CB_I128:CB_I128 + 128], start=True, tp=(0, RB))
            mm(bGA, PBS[bGA][RB:RB + 64, 128:256], xbt(64, 128),
               cbS[:, CB_I128:CB_I128 + 128], start=True, tp=(0, RB))
            xbc = XBcs[RB:RB + 64, 256 * par:256 * par + 256]
            for hh in range(2):
                vop("vector", "tensor_scalar", xbc[0:64, 64 * hh:64 * hh + 64],
                    PBS[bGA][RB:RB + 64, 64 * hh:64 * hh + 64], co_lo(hh),
                    None, ALU.mult, writes=(K_XBC,))
                vop("vector", "tensor_scalar",
                    xbc[0:64, 128 + 64 * hh:128 + 64 * hh + 64],
                    PBS[bGA][RB:RB + 64, 128 + 64 * hh:128 + 64 * hh + 64],
                    co_hi2(hh), None, ALU.mult, writes=(K_XBC,))

            yield
            if KSP < 5:
                return
            if phase == "A":
                zfill(bT, PBS[bT][0:128, 0:256], 256)
                for hh in range(2):
                    yl = z1(0)[:, 128 * hh:128 * hh + 64]
                    yh = z1(1)[:, 128 * hh:128 * hh + 64]
                    vl = z1(0)[:, 128 * hh + 64:128 * hh + 128]
                    vh = z1(1)[:, 128 * hh + 64:128 * hh + 128]
                    xcl = xbc[0:64, 64 * hh:64 * hh + 64]
                    xch = xbc[0:64, 128 + 64 * hh:128 + 64 * hh + 64]
                    mm(bT, PBS[bT][64 * hh:64 * hh + 64, 0:64], yl, xcl,
                       start=False, stop=False, reads=(K_XBC, K_Z1),
                       tp=(RB, 64 * hh))
                    mm(bT, PBS[bT][64 * hh:64 * hh + 64, 0:64], yh, xch,
                       start=False, stop=False, tp=(RB, 64 * hh))
                    mm(bT, PBS[bT][64 * hh:64 * hh + 64, 64:128], xcl, vl,
                       start=False, stop=False, tp=(RB, 64 * hh))
                    mm(bT, PBS[bT][64 * hh:64 * hh + 64, 64:128], xch, vh,
                       start=False, stop=False, tp=(RB, 64 * hh))
                if KSP < 6:
                    return
                dt = dTs[:, 64 * (2 * par + rh):64 * (2 * par + rh) + 64]
                copy_ev("vector", dt, PBS[bT][:, 0:64],
                        writes=(("dts", slot4),))
                arp = Arun[:, 64 * pr:64 * pr + 64]
                brp = Brun[:, 64 * pr:64 * pr + 64]
                for hh in range(2):
                    r0 = 64 * hh
                    mm(bT, PBS[bT][r0:r0 + 64, 128:192], dt[r0:r0 + 64, :],
                       arp[r0:r0 + 64, :], start=False, stop=False,
                       reads=(("dts", slot4), ("arun")), tp=(r0, r0))
                    mm(bT, PBS[bT][r0:r0 + 64, 192:256], dt[r0:r0 + 64, :],
                       brp[r0:r0 + 64, :], start=False, stop=False,
                       reads=(("brun"),), tp=(r0, r0))
                vop("vector", "tensor_sub", arp, arp, PBS[bT][:, 128:192],
                    writes=(("arun"),))
                vop("vector", "tensor_sub",
                    ttmp[:, 64 * slot4:64 * slot4 + 64],
                    brp, PBS[bT][:, 192:256], writes=(("ttm", slot4),))
                vop("vector", "tensor_add", brp,
                    ttmp[:, 64 * slot4:64 * slot4 + 64], PBS[bT][:, 64:128],
                    reads=(("ttm", slot4),), writes=(("brun"),))
            else:
                sbd = SbdT[:, 128 * slot4:128 * slot4 + 128]
                zfill(bT, PBS[bT][0:128, 0:448], 448)
                mm(bT, PBS[bT][0:128, 0:128], xct(0, 128), sbd,
                   start=False, stop=False, reads=(K_SBD,), tp=(0, 0))
                for hh in range(2):
                    mm(bT, PBS[bT][0:128, 64 * hh:64 * hh + 64], at_s(0, hh),
                       z1(0)[:, 64 * hh:64 * hh + 64], start=False, stop=False,
                       reads=(K_ATS, K_Z1), tp=(RB, 0))
                    mm(bT, PBS[bT][0:128, 64 * hh:64 * hh + 64], at_s(1, hh),
                       z1(1)[:, 64 * hh:64 * hh + 64], start=False, stop=False,
                       tp=(RB, 0))
                    mm(bT, PBS[bT][0:128, 128 + 64 * hh:128 + 64 * hh + 64],
                       at_d(0, hh), z1(0)[:, 64 * hh:64 * hh + 64],
                       start=False, stop=False, tp=(RB, 0))
                    mm(bT, PBS[bT][0:128, 128 + 64 * hh:128 + 64 * hh + 64],
                       at_d(1, hh), z1(1)[:, 64 * hh:64 * hh + 64],
                       start=False, stop=False, tp=(RB, 0))
                for hh in range(2):
                    mm(bT, PBS[bT][64 * hh:64 * hh + 64, 256:320],
                       xbc[0:64, 64 * hh:64 * hh + 64],
                       z1(0)[:, 64 * hh:64 * hh + 64], start=False, stop=False,
                       reads=(K_XBC,), tp=(RB, 64 * hh))
                    mm(bT, PBS[bT][64 * hh:64 * hh + 64, 256:320],
                       xbc[0:64, 128 + 64 * hh:128 + 64 * hh + 64],
                       z1(1)[:, 64 * hh:64 * hh + 64], start=False, stop=False,
                       tp=(RB, 64 * hh))
                od2 = odst[:, 128 * par:128 * par + 128]
                copy_ev("vector", od2, PBS[bT][:, 128:256],
                        writes=(("od", par),))
                osb = OUTsb[:, 128 * par:128 * par + 128]
                for hh in range(2):
                    vop("vector", "scalar_tensor_tensor",
                        osb[:, 64 * hh:64 * hh + 64],
                        od2[:, 64 * hh:64 * hh + 64], nco_full(hh),
                        PBS[bT][:, 64 * hh:64 * hh + 64], ALU.mult, ALU.add,
                        reads=(("od", par),), writes=(K_OSB,))
                vop("vector", "tensor_sub", S_all[:, 64 * pr:64 * pr + 64],
                    S_all[:, 64 * pr:64 * pr + 64], PBS[bT][:, 256:320],
                    writes=(("sall"),))
                mm(bT, PBS[bT][0:128, 320:448], osb,
                   cbS[:, CB_I128:CB_I128 + 128], start=False,
                   reads=(K_OSB,), tp=(0, 0))
                eng = "vector" if pr % 2 == 0 else "scalar"
                copy_ev(eng, ar(1024 * pr + TSC * s, 128), PBS[bT][:, 320:448],
                        writes=(("xcwt"),))

        def run_rolling(factories):
            todo = list(factories)
            active = []
            while todo and len(active) < 4:
                active.append(todo.pop(0)())
            while active:
                nxt = []
                for gg in active:
                    try:
                        next(gg)
                        nxt.append(gg)
                    except StopIteration:
                        if todo:
                            g2 = todo.pop(0)()
                            try:
                                next(g2)
                                nxt.append(g2)
                            except StopIteration:
                                pass
                active = nxt

        KP3 = int(os.environ.get("KP3", "9"))
        for half in range(2 if phases >= 3 else 0):
            build_xnT(HALF * half, HALF, 1024)
            project(wk, R0C)
            project(wv, R0C + 16384)
            ilr_proj(half)
            for s in range(SC_H if KP3 >= 2 else 0):
                co_build(half, s)
                if KP3 < 3:
                    continue
                np_ = NPAIR if KP3 >= 4 else 1
                run_rolling([lambda pr=pr: scan_pair_gen(pr, s, half, "A")
                             for pr in range(np_)])
            dma("sync", "SPB", xb_sp.ap()[half], ar(R0C, 16384),
                reads=(("proj"),))
            dma("sync", "SPA", xa_sp.ap()[half], ar(R0C + 16384, 16384),
                reads=(("proj"),))
        barrier()

        for pr in range(NPAIR if phases >= 4 else 0):
            for hh in range(2):
                r0 = 64 * hh
                mm(6, pb6[r0:r0 + 64, 0:64],
                   Arun[r0:r0 + 64, 64 * pr:64 * pr + 64], idf[r0:r0 + 64, 0:64],
                   start=True, reads=(("arun"),))
            copy_ev("vector", CCB[:, 128 * pr:128 * pr + 64],
                    pb6[:, 0:64], banks=(6,), writes=(("ccb"),))
            copy_ev("vector", CCB[:, 128 * pr + 64:128 * pr + 128],
                    Brun[:, 64 * pr:64 * pr + 64], reads=(("brun"),),
                    writes=(("ccb"),))
        if phases >= 4:
          dma("sync", "CCIN", cc_in.ap(), CCB[:, 0:2048], reads=(("ccb"),))
          p.wait("gpsimd", p.tok("CCIN"))
          t_cc = p.inc("cc", 1)
          def fn_cc(e):
              e.collective_compute("AllGather", ALU.bypass,
                                   replica_groups=[list(range(NCORE))],
                                   ins=[cc_in.ap().opt()],
                                   outs=[cc_out.ap().opt()]).then_inc(SEM["cc"], 1)
          p.ops["gpsimd"].append(fn_cc)
          p.wait("sync", p.tok("cc"))
          for j in range(NCORE - 1):
              tja = dma("sync", "CCA", CCB[:, 0:1024],
                        cc_out.ap()[128 * j:128 * j + 128, 0:1024],
                        writes=(("ccb"),))
              tjb = dma("sync", "CCB", CCB[:, 1024:2048],
                        cc_out.ap()[128 * j:128 * j + 128, 1024:2048],
                        writes=(("ccb"),))
              mj = pmS[:, j:j + 1]
              mj1 = pmS[:, 8 + j:8 + j + 1]
              for pr in range(NPAIR):
                  atp = CCB[:, 128 * pr:128 * pr + 64]
                  btp = CCB[:, 128 * pr + 64:128 * pr + 128]
                  sp = S_all[:, 64 * pr:64 * pr + 64]
                  mm(6, pb6[0:128, 64:128],
                     cbS[32:33, CB_N128ROW:CB_N128ROW + 128],
                     cbS[32:33, 0:64], start=True, stop=False, tp=(32, 0))
                  for hh in range(2):
                      r0 = 64 * hh
                      mm(6, pb6[r0:r0 + 64, 64:128], atp[r0:r0 + 64, :],
                         sp[r0:r0 + 64, :], start=False, stop=False,
                         reads=(("ccb"), ("sall")), tp=(r0, r0))
                      mm(6, pb6[r0:r0 + 64, 64:128], idf[r0:r0 + 64, 0:64],
                         btp[r0:r0 + 64, :], start=False, stop=False,
                         reads=(("ccb"),), tp=(r0, r0))
                  vop("vector", "tensor_scalar", ttmp[:, 0:64], sp, mj1, None,
                      ALU.mult, reads=(("sall"),), writes=(("ttm", 0),))
                  vop("vector", "scalar_tensor_tensor", sp, pb6[:, 64:128], mj,
                      ttmp[:, 0:64], ALU.mult, ALU.add, banks=(6,),
                      reads=(("ttm", 0),), writes=(("sall"),))

        for half in range(2 if phases >= 5 else 0):
            for qh in range(2):
                build_xnT(HALF * half + 512 * qh, 512, 512, R0C + 24576)
                project(wq, R0C + 16384, xn_base=R0C + 24576, n_tok=512)
                xbv = xb_sp.ap()[half].rearrange(
                    "p (oc c) -> p oc c", c=1024)[:, :, 512 * qh:512 * qh + 512]
                xav = xa_sp.ap()[half].rearrange(
                    "p (oc c) -> p oc c", c=1024)[:, :, 512 * qh:512 * qh + 512]
                dma("sync", "RLD",
                    ar(R0C, 8192).rearrange("p (oc c) -> p oc c", c=512),
                    xbv, writes=(("proj"),))
                dma("sync", "RLD",
                    ar(R0C + 8192, 8192).rearrange("p (oc c) -> p oc c", c=512),
                    xav, writes=(("proj"),))
                for siq in range(4):
                    s = 4 * qh + siq
                    co_build(half, s)
                    run_rolling([lambda pr=pr: scan_pair_gen(pr, s, half, "B")
                                 for pr in range(NPAIR)])
                barrier()
            for oc in range(16):
                b0, b1 = (0, 1) if oc % 2 == 0 else (2, 3)
                wslot = (oc // 2) % 2
                if oc % 2 == 0:
                    dma("gpsimd", "WB%d" % wslot,
                        wbufs[:, 4096 * wslot:4096 * wslot + 4096]
                        .rearrange("p (i c) -> p i c", c=2048),
                        wo.tensor.ap()[oc:oc + 2].transpose([1, 0, 2]),
                        writes=(("wb", wslot),))
                wbuf = wbufs[:, 4096 * wslot + 2048 * (oc % 2):
                             4096 * wslot + 2048 * (oc % 2) + 2048]
                for dc in range(NDC):
                    mm(b0, PBS[b0][:, 0:512], wbuf[:, 128 * dc:128 * dc + 128],
                       ar(1024 * dc, 512), start=(dc == 0), stop=(dc == NDC - 1),
                       reads=(("wb", wslot), ("xcwt")))
                    mm(b1, PBS[b1][:, 0:512], wbuf[:, 128 * dc:128 * dc + 128],
                       ar(1024 * dc + 512, 512), start=(dc == 0),
                       stop=(dc == NDC - 1), reads=(("wb", wslot),))
                eng = "vector" if oc % 2 == 0 else "scalar"
                copy_ev(eng, ar(R0C + 1024 * oc, 512), PBS[b0][:, 0:512],
                        banks=(b0,), writes=(("wost", oc),))
                copy_ev(eng, ar(R0C + 1024 * oc + 512, 512), PBS[b1][:, 0:512],
                        banks=(b1,), writes=(("wost", oc),))
                dma("sync", "ss",
                    s_sp.ap()[128 * oc:128 * oc + 128,
                              HALF * half:HALF * half + 1024],
                    ar(R0C + 1024 * oc, 1024), reads=(("wost", oc),))
            barrier()
        barrier()

        for tt in range(NTOK if phases >= 6 else 0):
            par = tt % 2
            xs = XS(par)
            gt = ar(4096 * par, 2048)
            st = ar(4096 * par + 2048, 2048)
            t_x = dma("sync", "XS%d" % par, xs, x_in[128 * tt:128 * tt + 128, :],
                      writes=(("xst", par),))
            t_g = dma("sync", "CG%d" % par, gt,
                      g_sp.ap()[128 * tt:128 * tt + 128, :],
                      writes=(("cg", par),))
            t_s = dma("scalar", "CS%d" % par, st,
                      s_sp.ap()[:, 128 * tt:128 * tt + 128],
                      transpose=True, writes=(("cs", par),))
            for ch in range(4):
                tv = vop("vector", "tensor_mul", STA_C,
                         st[:, 512 * ch:512 * ch + 512],
                         gt[:, 512 * ch:512 * ch + 512],
                         reads=(("cg", par), ("cs", par)), writes=(("sta"),),
                         wait=merge(t_g, t_s))
                vop("vector", "tensor_add", xs[:, 512 * ch:512 * ch + 512],
                    xs[:, 512 * ch:512 * ch + 512], STA_C,
                    reads=(("sta"), ("xst", par)), writes=(("xst", par),),
                    wait=merge(tv, t_x))
            dma("sync", "Y%d" % par, y_out[128 * tt:128 * tt + 128, :], xs,
                reads=(("xst", par),))
        barrier()

        with nc.Block() as block:
            def run(eng_obj, key):
                for op in p.ops[key]:
                    if isinstance(op, tuple) and op[0] == "wait":
                        eng_obj.wait_ge(SEM[op[1]], op[2])
                    elif isinstance(op, tuple) and op[0] == "dma":
                        op[2](eng_obj)
                    else:
                        op(eng_obj)
            @block.tensor
            def _(t):
                run(t, "tensor")
            @block.vector
            def _(v):
                run(v, "vector")
            @block.scalar
            def _(s):
                run(s, "scalar")
            @block.gpsimd
            def _(g):
                run(g, "gpsimd")
            @block.sync
            def _(s):
                run(s, "sync")
    return nc


def _retile_proj(w):
    wt = np.ascontiguousarray(w.T.astype(np.float32))
    r = wt.reshape(16, 128, 16, 128)
    r = np.ascontiguousarray(r.transpose(2, 1, 0, 3))
    return r.reshape(16, 128, 2048)


def _retile_gu(w):
    wt = np.ascontiguousarray(w.T)
    r = wt.reshape(16, 128, 64, 128)
    r = np.ascontiguousarray(r.transpose(2, 1, 0, 3))
    return r.reshape(64, 128, 2048)


def _retile_dw(w):
    wt = np.ascontiguousarray(w.T)
    return np.ascontiguousarray(wt.reshape(64, 128, 2048))


def _consts():
    cb = np.zeros((128, 904), np.float32)
    cb[:, CB_I128:CB_I128 + 128] = np.eye(128)
    cb[:, CB_NEGI128:CB_NEGI128 + 128] = -np.eye(128)
    strict = np.zeros((128, 128), np.float32)
    diag = np.zeros((128, 128), np.float32)
    for c in range(8):
        strict[:16 * c, 16 * c:16 * c + 16] = 1.0
        diag[16 * c:16 * c + 16, 16 * c:16 * c + 16] = np.triu(
            np.ones((16, 16), np.float32))
    for rr in (0, 64):
        cb[rr:rr + 64, CB_STRICT:CB_STRICT + 128] = strict[0:64]
        cb[rr:rr + 64, CB_STRICT_HI:CB_STRICT_HI + 128] = strict[64:128]
        cb[rr:rr + 64, CB_DIAG:CB_DIAG + 128] = diag[0:64]
        cb[rr:rr + 64, CB_DIAG_HI:CB_DIAG_HI + 128] = diag[64:128]
    cb[0:1, CB_N128ROW:CB_N128ROW + 128] = -128.0
    cb[0, CB_ONE] = 1.0
    cb[:, CB_N128COL] = -128.0
    cf = np.zeros((128, 8), np.float32)
    invk = 1.0 / np.arange(1, CS + 1, dtype=np.float32)
    cf[:, CF_INVK] = np.tile(invk, 8) / HF
    cf[:, CF_NEGINVK] = -cf[:, CF_INVK]
    cf[:, CF_EPS] = 1e-5
    cf[:, CF_INVD] = 1.0 / D
    return cb.astype(bfl), cf


_prog_cache = {}


def kernel(L0_attn_hidden_states, L0_attn_Wq, L0_attn_Wk, L0_attn_Wv,
           L0_attn_Wo, L0_attn_W_ilr, L0_attn_b_ilr, L0_attn_W1,
           L0_ffn_gate_w, L0_ffn_gate_scale, L0_ffn_up_w, L0_ffn_up_scale,
           L0_ffn_down_w, L0_ffn_down_scale):
    x = np.asarray(L0_attn_hidden_states, np.float32).reshape(L, D)
    cbv, cfv = _consts()
    wilr = np.ascontiguousarray(
        np.asarray(L0_attn_W_ilr, np.float32).T.reshape(16, 128, 32)
        .transpose(1, 0, 2)).reshape(128, 512)
    w1v = np.ascontiguousarray(
        np.asarray(L0_attn_W1, np.float32).reshape(16, 128, 64)
        .transpose(1, 0, 2)).reshape(128, 1024)
    base = dict(
        wq=_retile_proj(np.asarray(L0_attn_Wq, np.float32)),
        wk=_retile_proj(np.asarray(L0_attn_Wk, np.float32)),
        wv=_retile_proj(np.asarray(L0_attn_Wv, np.float32)),
        wo=_retile_proj(np.asarray(L0_attn_Wo, np.float32)),
        wilr=wilr,
        bilr=np.asarray(L0_attn_b_ilr, np.float32).reshape(32, 1),
        w1=w1v,
        guw=np.ascontiguousarray(np.concatenate(
            [_retile_gu(np.asarray(L0_ffn_gate_w, np.int32)),
             _retile_gu(np.asarray(L0_ffn_up_w, np.int32))], axis=2)),
        dw=_retile_dw(np.asarray(L0_ffn_down_w, np.int32)),
        gs_t=np.ascontiguousarray(
            np.asarray(L0_ffn_gate_scale, np.float32).reshape(64, 128).T),
        us_t=np.ascontiguousarray(
            np.asarray(L0_ffn_up_scale, np.float32).reshape(64, 128).T),
        ds_b=np.ascontiguousarray(np.broadcast_to(
            np.asarray(L0_ffn_down_scale, np.float32)[None, :], (128, 2048))),
        cb=cbv, cf=cfv,
        i64f=np.concatenate(
            [np.concatenate([np.eye(64, dtype=np.float32)] * 2, 0),
             np.concatenate([np.eye(32, dtype=np.float32)] * 4, 0)], 1),
    )
    in_maps = []
    for core in range(NCORE):
        pm = np.zeros((128, 16), np.float32)
        for j in range(NCORE - 1):
            m = 1.0 if j < core else 0.0
            pm[:, j] = m
            pm[:, 8 + j] = 1.0 - m
        d = dict(base)
        d["x"] = np.ascontiguousarray(x[TOKC * core:TOKC * (core + 1)])
        d["pmask"] = pm
        in_maps.append(d)

    if "nc" not in _prog_cache:
        _prog_cache["nc"] = build_program(int(os.environ.get("KPHASES", "99")))
    nc = _prog_cache["nc"]
    res = run_bass_kernel_spmd(nc, in_maps, core_ids=list(range(NCORE)))
    out = np.concatenate([res.results[i]["y"] for i in range(NCORE)], axis=0)
    return out.reshape(1, L, D).astype(np.float32)
